# version 01
# baseline (speedup 1.0000x reference)
import numpy as np
import ml_dtypes
from contextlib import ExitStack
import concourse.bass as bass
import concourse.mybir as mybir
from concourse.bass_utils import run_bass_kernel_spmd

F32 = mybir.dt.float32
BF16 = mybir.dt.bfloat16
AF = mybir.ActivationFunctionType
ALU = mybir.AluOpType
AX = mybir.AxisListType

H = 4096
D = 8192
NS = 16
R = 256
T = 2048
LN_EPS = 1e-5
RMS_EPS = 1e-6

CONST_VALS = [LN_EPS, RMS_EPS, 1.0, -0.5, -1.0, 1.0 / 4096, 1.0 / 256, 1.0 / 16]

NKT = H // 128
NOT = D // 128
NTT = T // 128
NTB = T // 512


def build_nc():
    nc = bass.Bass("TRN2", target_bir_lowering=False)

    x = nc.dram_tensor("x", [T, H], F32, kind="ExternalInput").ap()
    wu_t = nc.dram_tensor("wu_t", [NOT, NKT, 128, 128], F32, kind="ExternalInput").ap()
    wg_t = nc.dram_tensor("wg_t", [NOT, NKT, 128, 128], F32, kind="ExternalInput").ap()
    wx_T = nc.dram_tensor("wx_T", [D, 288], F32, kind="ExternalInput").ap()
    wdt_t = nc.dram_tensor("wdt_t", [2, NOT, 128, 128], F32, kind="ExternalInput").ap()
    wout_T = nc.dram_tensor("wout_T", [D, H], F32, kind="ExternalInput").ap()
    wbase_T = nc.dram_tensor("wbase_T", [H, H], F32, kind="ExternalInput").ap()
    wrouter_T = nc.dram_tensor("wrouter_T", [H, 8], F32, kind="ExternalInput").ap()
    acomb_T = nc.dram_tensor("acomb_T", [H, 128], F32, kind="ExternalInput").ap()
    bcomb_T = nc.dram_tensor("bcomb_T", [128, H], F32, kind="ExternalInput").ap()
    b_base = nc.dram_tensor("b_base", [1, H], F32, kind="ExternalInput").ap()
    conv_w = nc.dram_tensor("conv_w", [128, NOT * 4], F32, kind="ExternalInput").ap()
    conv_b = nc.dram_tensor("conv_b", [128, NOT], F32, kind="ExternalInput").ap()
    dt_bias = nc.dram_tensor("dt_bias", [128, NOT], F32, kind="ExternalInput").ap()
    d_par = nc.dram_tensor("d_par", [128, NOT], F32, kind="ExternalInput").ap()
    a_par = nc.dram_tensor("a_par", [128, NOT * NS], F32, kind="ExternalInput").ap()
    dt_ln_w = nc.dram_tensor("dt_ln_w", [128, 2], F32, kind="ExternalInput").ap()
    b_ln_w = nc.dram_tensor("b_ln_w", [16, 1], F32, kind="ExternalInput").ap()
    c_ln_w = nc.dram_tensor("c_ln_w", [16, 1], F32, kind="ExternalInput").ap()
    ident_f = nc.dram_tensor("ident_f", [128, 128], F32, kind="ExternalInput").ap()
    ident_b = nc.dram_tensor("ident_b", [128, 128], BF16, kind="ExternalInput").ap()
    ones_f = nc.dram_tensor("ones_f", [1, 128], F32, kind="ExternalInput").ap()
    cvals = nc.dram_tensor("cvals", [128, 16], F32, kind="ExternalInput").ap()

    y_out = nc.dram_tensor("y_out", [T, H], F32, kind="ExternalOutput").ap()

    wu_bf = nc.dram_tensor("wu_bf", [NOT, NKT, 128, 128], BF16, kind="Internal").ap()
    wg_bf = nc.dram_tensor("wg_bf", [NOT, NKT, 128, 128], BF16, kind="Internal").ap()
    wx_bf = nc.dram_tensor("wx_bf", [D, 288], BF16, kind="Internal").ap()
    wdt_bf = nc.dram_tensor("wdt_bf", [2, NOT, 128, 128], BF16, kind="Internal").ap()
    wout_bf = nc.dram_tensor("wout_bf", [D, H], BF16, kind="Internal").ap()
    wbase_bf = nc.dram_tensor("wbase_bf", [H, H], BF16, kind="Internal").ap()
    wrouter_bf = nc.dram_tensor("wrouter_bf", [H, 8], BF16, kind="Internal").ap()
    acomb_bf = nc.dram_tensor("acomb_bf", [H, 128], BF16, kind="Internal").ap()
    bcomb_bf = nc.dram_tensor("bcomb_bf", [128, H], BF16, kind="Internal").ap()
    u_f = nc.dram_tensor("u_f", [D, T], F32, kind="Internal").ap()
    ub_t = nc.dram_tensor("ub_t", [NTT, NOT, 128, 128], BF16, kind="Internal").ap()
    sg_f = nc.dram_tensor("sg_f", [D, T], F32, kind="Internal").ap()
    delta_f = nc.dram_tensor("delta_f", [D, T], F32, kind="Internal").ap()
    yg_td = nc.dram_tensor("yg_td", [NTT, NOT, 128, 128], BF16, kind="Internal").ap()
    x2_f = nc.dram_tensor("x2_f", [T, H], F32, kind="Internal").ap()
    bt_d = nc.dram_tensor("bt_d", [16, T], F32, kind="Internal").ap()
    ct_d = nc.dram_tensor("ct_d", [16, T], F32, kind="Internal").ap()

    G = [0]

    with nc.semaphore("gs") as gs:

        def fin(inst, k=1):
            inst.then_inc(gs, k)
            G[0] += k

        def wt(eng):
            if G[0] > 0:
                eng.wait_ge(gs, G[0])

        def dma(out, in_):
            wt(nc.sync)
            fin(nc.sync.dma_start(out=out, in_=in_), 16)

        def sca(**kw):
            wt(nc.scalar)
            fin(nc.scalar.activation(**kw))

        def scopy(out, in_):
            wt(nc.scalar)
            fin(nc.scalar.copy(out=out, in_=in_))

        def vts(**kw):
            wt(nc.vector)
            fin(nc.vector.tensor_scalar(**kw))

        def vtt(out, in0, in1, op):
            wt(nc.vector)
            fin(nc.vector.tensor_tensor(out=out, in0=in0, in1=in1, op=op))

        def vstt(**kw):
            wt(nc.vector)
            fin(nc.vector.scalar_tensor_tensor(**kw))

        _es1 = ExitStack()
        c_in = _es1.enter_context(nc.sbuf_tensor("c_in", [128, 8192], F32))
        c_out = _es1.enter_context(nc.sbuf_tensor("c_out", [128, 8192], BF16))
        if True:
            def cast_flat(src, dst, nelem, pat):
                CH = 128 * 8192
                nfull = nelem // CH
                sv = src.rearrange(pat)
                dv = dst.rearrange(pat)
                for c in range(nfull):
                    s3 = sv[c * CH:(c + 1) * CH].rearrange("(p f) -> p f", p=128)
                    d3 = dv[c * CH:(c + 1) * CH].rearrange("(p f) -> p f", p=128)
                    dma(c_in[:], s3)
                    scopy(c_out[:], c_in[:])
                    dma(d3, c_out[:])
                rem = nelem - nfull * CH
                if rem:
                    fr = rem // 128
                    s3 = sv[nfull * CH:].rearrange("(p f) -> p f", p=128)
                    d3 = dv[nfull * CH:].rearrange("(p f) -> p f", p=128)
                    dma(c_in[:, :fr], s3)
                    scopy(c_out[:, :fr], c_in[:, :fr])
                    dma(d3, c_out[:, :fr])

            p4 = "a b c d -> (a b c d)"
            p2 = "a b -> (a b)"
            cast_flat(wu_t, wu_bf, NOT * NKT * 128 * 128, p4)
            cast_flat(wg_t, wg_bf, NOT * NKT * 128 * 128, p4)
            cast_flat(wx_T, wx_bf, D * 288, p2)
            cast_flat(wout_T, wout_bf, D * H, p2)
            cast_flat(wbase_T, wbase_bf, H * H, p2)
            cast_flat(wrouter_T, wrouter_bf, H * 8, p2)
            cast_flat(acomb_T, acomb_bf, H * 128, p2)
            cast_flat(bcomb_T, bcomb_bf, 128 * H, p2)

        _es1.close()
        _es2 = ExitStack()
        wd_in = _es2.enter_context(nc.sbuf_tensor("wd_in", [128, 1024], F32))
        wd_out = _es2.enter_context(nc.sbuf_tensor("wd_out", [128, 1024], BF16))
        s_lnw = _es2.enter_context(nc.sbuf_tensor("s_lnw", [128, 2], F32))
        if True:
            dma(s_lnw[:], dt_ln_w)
            for rk in range(2):
                for g in range(NOT // 8):
                    src = wdt_t[rk, g * 8:(g + 1) * 8].rearrange("t p d -> p t d")
                    dst = wdt_bf[rk, g * 8:(g + 1) * 8].rearrange("t p d -> p t d")
                    dma(wd_in[:].rearrange("p (t d) -> p t d", t=8), src)
                    vts(out=wd_out[:], in0=wd_in[:], scalar1=s_lnw[:, rk:rk + 1],
                        scalar2=None, op0=ALU.mult)
                    dma(dst, wd_out[:].rearrange("p (t d) -> p t d", t=8))

        _es2.close()
        _es3 = ExitStack()
        s_idf = _es3.enter_context(nc.sbuf_tensor("s_idf", [128, 128], F32))
        s_idb = _es3.enter_context(nc.sbuf_tensor("s_idb", [128, 128], BF16))
        s_ones = _es3.enter_context(nc.sbuf_tensor("s_ones", [1, 128], F32))
        col1 = _es3.enter_context(nc.sbuf_tensor("col1", [128, 1], F32))
        col2 = _es3.enter_context(nc.sbuf_tensor("col2", [128, 1], F32))
        col3 = _es3.enter_context(nc.sbuf_tensor("col3", [128, 1], F32))
        col4 = _es3.enter_context(nc.sbuf_tensor("col4", [128, 1], F32))
        s_cv = _es3.enter_context(nc.sbuf_tensor("s_cv", [128, 16], F32))
        if True:
            dma(s_idf[:], ident_f)
            dma(s_idb[:], ident_b)
            dma(s_ones[:], ones_f)
            dma(s_cv[:], cvals)
            for ci, cval in enumerate(CONST_VALS):
                nc.const_aps.register(F32, cval, s_cv[:, ci:ci + 1])

            def layernorm_transpose(src_rows, dstT, p1x, p1sq, p1n, ps_bf):
                dstv = dstT.rearrange("p (k t) -> p k t", k=NKT)
                for tt in range(NTT):
                    dma(p1x[:], src_rows[tt * 128:(tt + 1) * 128, :])
                    wt(nc.vector)
                    fin(nc.vector.tensor_reduce(out=col1[:], in_=p1x[:], axis=AX.X, op=ALU.add))
                    wt(nc.scalar)
                    fin(nc.scalar.mul(col1[:], col1[:], 1.0 / H))
                    sca(out=p1sq[:], in_=p1x[:, 0:2048], func=AF.Square, accum_out=col2[:])
                    sca(out=p1sq[:], in_=p1x[:, 2048:4096], func=AF.Square, accum_out=col4[:])
                    vtt(col2[:], col2[:], col4[:], ALU.add)
                    vts(out=col3[:], in0=col1[:], scalar1=col1[:, 0:1], scalar2=None,
                        op0=ALU.mult)
                    vstt(out=col2[:], in0=col2[:], scalar=1.0 / H, in1=col3[:],
                         op0=ALU.mult, op1=ALU.subtract)
                    sca(out=col2[:], in_=col2[:], func=AF.Ln, bias=LN_EPS)
                    sca(out=col2[:], in_=col2[:], func=AF.Exp, scale=-0.5)
                    vstt(out=p1n[:], in0=p1x[:], scalar=col1[:, 0:1],
                         in1=col2[:, 0:1].broadcast_to([128, H]),
                         op0=ALU.subtract, op1=ALU.mult)
                    for hq in range(8):
                        wt(nc.tensor)
                        for j in range(4):
                            hb = hq * 4 + j
                            inst = nc.tensor.transpose(out=ps_bf[:, j * 128:(j + 1) * 128],
                                                       in_=p1n[:, hb * 128:(hb + 1) * 128],
                                                       identity=s_idb[:])
                        fin(inst)
                        scopy(dstv[:, hq * 4:(hq + 1) * 4, tt * 128:(tt + 1) * 128],
                              ps_bf[:].rearrange("p (j q) -> p j q", j=4))

            with nc.sbuf_tensor("lnxT", [128, NKT * T], BF16) as lnxT:
                _es4 = ExitStack()
                p1x = _es4.enter_context(nc.sbuf_tensor("p1x", [128, H], F32))
                p1sq = _es4.enter_context(nc.sbuf_tensor("p1sq", [128, 2048], F32))
                p1n = _es4.enter_context(nc.sbuf_tensor("p1n", [128, H], BF16))
                ps_bf1 = _es4.enter_context(nc.psum_tensor("ps_bf1", [128, 512], BF16))
                if True:
                    layernorm_transpose(x, lnxT, p1x, p1sq, p1n, ps_bf1)

                _es4.close()
                _es5 = ExitStack()
                s_convw = _es5.enter_context(nc.sbuf_tensor("s_convw", [128, NOT * 4], F32))
                s_convb = _es5.enter_context(nc.sbuf_tensor("s_convb", [128, NOT], F32))
                sw = _es5.enter_context(nc.sbuf_tensor("sw", [128, NKT * 128], BF16))
                raw = _es5.enter_context(nc.sbuf_tensor("raw", [128, T + 3], F32))
                su = _es5.enter_context(nc.sbuf_tensor("su", [128, T], F32))
                sub = _es5.enter_context(nc.sbuf_tensor("sub", [128, T], BF16))
                ps_u = _es5.enter_context(nc.psum_tensor("ps_u", [128, 512], F32))
                if True:
                    dma(s_convw[:], conv_w)
                    dma(s_convb[:], conv_b)
                    lv = lnxT.rearrange("p (k t) -> p k t", k=NKT)
                    for wsrc, do_conv in ((wu_bf, True), (wg_bf, False)):
                        for ot in range(NOT):
                            dma(sw[:].rearrange("p (k o) -> p k o", k=NKT),
                                wsrc[ot].rearrange("k p o -> p k o"))
                            swv = sw.rearrange("p (k o) -> p k o", k=NKT)
                            for tb in range(NTB):
                                wt(nc.tensor)
                                for k in range(NKT):
                                    inst = nc.tensor.matmul(
                                        out=ps_u[:], lhsT=swv[:, k, :],
                                        rhs=lv[:, k, tb * 512:(tb + 1) * 512],
                                        start=(k == 0), stop=(k == NKT - 1))
                                fin(inst)
                                if do_conv:
                                    scopy(raw[:, 3 + tb * 512:3 + (tb + 1) * 512], ps_u[:])
                                else:
                                    sca(out=su[:, tb * 512:(tb + 1) * 512], in_=ps_u[:],
                                        func=AF.Silu)
                            if do_conv:
                                wt(nc.vector)
                                fin(nc.vector.memset(raw[:, 0:3], 0.0))
                                vts(out=su[:], in0=raw[:, 0:T],
                                    scalar1=s_convw[:, ot * 4:ot * 4 + 1],
                                    scalar2=None, op0=ALU.mult)
                                for kk in range(1, 4):
                                    vstt(out=su[:], in0=raw[:, kk:T + kk],
                                         scalar=s_convw[:, ot * 4 + kk:ot * 4 + kk + 1],
                                         in1=su[:], op0=ALU.mult, op1=ALU.add)
                                sca(out=su[:], in_=su[:], func=AF.Silu,
                                    bias=s_convb[:, ot:ot + 1])
                                scopy(sub[:], su[:])
                                dma(u_f[ot * 128:(ot + 1) * 128, :], su[:])
                                dma(ub_t[:, ot].rearrange("t p q -> p t q"),
                                    sub[:].rearrange("p (t q) -> p t q", t=NTT))
                            else:
                                dma(sg_f[ot * 128:(ot + 1) * 128, :], su[:])

                _es5.close()
            _es6 = ExitStack()
            ssm_sb = _es6.enter_context(nc.sbuf_tensor("ssm_sb", [128, NTT * 288], F32))
            dtnT_sb = _es6.enter_context(nc.sbuf_tensor("dtnT_sb", [128, 2 * T], BF16))
            BT_sb = _es6.enter_context(nc.sbuf_tensor("BT_sb", [16, T], F32))
            CT_sb = _es6.enter_context(nc.sbuf_tensor("CT_sb", [16, T], F32))
            if True:
                _es7 = ExitStack()
                syg = _es7.enter_context(nc.sbuf_tensor("syg", [128, NOT * 128], BF16))
                swxa = _es7.enter_context(nc.sbuf_tensor("swxa", [128, NOT * 288], BF16))
                ps_s = _es7.enter_context(nc.psum_tensor("ps_s", [128, 512], F32))
                if True:
                    dma(swxa[:].rearrange("p (a n) -> p a n", a=NOT),
                        wx_bf.rearrange("(a p) n -> p a n", p=128))
                    swxv = swxa.rearrange("p (a n) -> p a n", a=NOT)
                    for tt in range(NTT):
                        dma(syg[:].rearrange("p (a q) -> p a q", a=NOT),
                            ub_t[tt].rearrange("a p q -> p a q"))
                        sygv = syg.rearrange("p (a q) -> p a q", a=NOT)
                        wt(nc.tensor)
                        for dk in range(NOT):
                            inst = nc.tensor.matmul(out=ps_s[:, 0:288], lhsT=sygv[:, dk, :],
                                                    rhs=swxv[:, dk, :],
                                                    start=(dk == 0), stop=(dk == NOT - 1))
                        fin(inst)
                        scopy(ssm_sb[:, tt * 288:(tt + 1) * 288], ps_s[:, 0:288])

                _es7.close()
                _es8 = ExitStack()
                s_blnw = _es8.enter_context(nc.sbuf_tensor("s_blnw", [16, 1], F32))
                s_clnw = _es8.enter_context(nc.sbuf_tensor("s_clnw", [16, 1], F32))
                p4s = _es8.enter_context(nc.sbuf_tensor("p4s", [128, 256], F32))
                p4b = _es8.enter_context(nc.sbuf_tensor("p4b", [128, 256], BF16))
                p4f = _es8.enter_context(nc.sbuf_tensor("p4f", [128, 16], F32))
                ps_bf2 = _es8.enter_context(nc.psum_tensor("ps_bf2", [128, 128], BF16))
                ps_t32 = _es8.enter_context(nc.psum_tensor("ps_t32", [128, 128], F32))
                if True:
                    dma(s_blnw[:], b_ln_w)
                    dma(s_clnw[:], c_ln_w)
                    for tt in range(NTT):
                        base = ssm_sb[:, tt * 288:(tt + 1) * 288]
                        sca(out=p4s[:], in_=base[:, 0:256], func=AF.Square, accum_out=col1[:])
                        sca(out=col1[:], in_=col1[:], func=AF.Ln, scale=1.0 / R, bias=RMS_EPS)
                        sca(out=col1[:], in_=col1[:], func=AF.Exp, scale=-0.5)
                        vts(out=p4b[:], in0=base[:, 0:256], scalar1=col1[:, 0:1],
                            scalar2=None, op0=ALU.mult)
                        for rk in range(2):
                            wt(nc.tensor)
                            fin(nc.tensor.transpose(out=ps_bf2[:],
                                                    in_=p4b[:, rk * 128:(rk + 1) * 128],
                                                    identity=s_idb[:]))
                            scopy(dtnT_sb[:, rk * T + tt * 128:rk * T + (tt + 1) * 128],
                                  ps_bf2[:])
                        for off, dstT in ((256, BT_sb), (272, CT_sb)):
                            sca(out=p4f[:], in_=base[:, off:off + 16], func=AF.Square,
                                accum_out=col1[:])
                            sca(out=col1[:], in_=col1[:], func=AF.Ln, scale=1.0 / NS,
                                bias=RMS_EPS)
                            sca(out=col1[:], in_=col1[:], func=AF.Exp, scale=-0.5)
                            vts(out=p4f[:], in0=base[:, off:off + 16], scalar1=col1[:, 0:1],
                                scalar2=None, op0=ALU.mult)
                            wt(nc.tensor)
                            fin(nc.tensor.transpose(out=ps_t32[0:16, :], in_=p4f[:],
                                                    identity=s_idf[:]))
                            scopy(dstT[:, tt * 128:(tt + 1) * 128], ps_t32[0:16, :])
                    vts(out=BT_sb[:], in0=BT_sb[:], scalar1=s_blnw[:, 0:1], scalar2=None,
                        op0=ALU.mult)
                    vts(out=CT_sb[:], in0=CT_sb[:], scalar1=s_clnw[:, 0:1], scalar2=None,
                        op0=ALU.mult)
                    dma(bt_d, BT_sb[:])
                    dma(ct_d, CT_sb[:])

                _es8.close()
                _es9 = ExitStack()
                s_dtb = _es9.enter_context(nc.sbuf_tensor("s_dtb", [128, NOT], F32))
                swdt = _es9.enter_context(nc.sbuf_tensor("swdt", [128, 2 * NOT * 128], BF16))
                d1 = _es9.enter_context(nc.sbuf_tensor("d1", [128, 512], F32))
                d2 = _es9.enter_context(nc.sbuf_tensor("d2", [128, 512], F32))
                d3 = _es9.enter_context(nc.sbuf_tensor("d3", [128, 512], F32))
                ps_d = _es9.enter_context(nc.psum_tensor("ps_d", [128, 512], F32))
                if True:
                    dma(s_dtb[:], dt_bias)
                    dma(swdt[:].rearrange("p (r a d) -> p r a d", r=2, a=NOT),
                        wdt_bf.rearrange("r a p d -> p r a d"))
                    swdtv = swdt.rearrange("p (r a d) -> p r a d", r=2, a=NOT)
                    dtv = dtnT_sb.rearrange("p (r t) -> p r t", r=2)
                    for dc in range(NOT):
                        for tb in range(NTB):
                            wt(nc.tensor)
                            for rk in range(2):
                                inst = nc.tensor.matmul(out=ps_d[:], lhsT=swdtv[:, rk, dc, :],
                                                        rhs=dtv[:, rk, tb * 512:(tb + 1) * 512],
                                                        start=(rk == 0), stop=(rk == 1))
                            fin(inst)
                            bias = s_dtb[:, dc:dc + 1]
                            sca(out=d1[:], in_=ps_d[:], func=AF.Abs, bias=bias)
                            sca(out=d1[:], in_=d1[:], func=AF.Exp, scale=-1.0)
                            sca(out=d1[:], in_=d1[:], func=AF.Ln, bias=1.0)
                            sca(out=d2[:], in_=ps_d[:], func=AF.Relu, bias=bias)
                            vtt(d3[:], d1[:], d2[:], ALU.add)
                            dma(delta_f[dc * 128:(dc + 1) * 128, tb * 512:(tb + 1) * 512],
                                d3[:])

                _es9.close()
            _es6.close()
            _es10 = ExitStack()
            s_apar = _es10.enter_context(nc.sbuf_tensor("s_apar", [128, NOT * NS], F32))
            s_dpar = _es10.enter_context(nc.sbuf_tensor("s_dpar", [128, NOT], F32))
            btrow = _es10.enter_context(nc.sbuf_tensor("btrow", [1, T], F32))
            ctrow = _es10.enter_context(nc.sbuf_tensor("ctrow", [1, T], F32))
            sdel = _es10.enter_context(nc.sbuf_tensor("sdel", [128, 4 * T], F32))
            sy = _es10.enter_context(nc.sbuf_tensor("sy", [128, 4 * T], F32))
            sdu = _es10.enter_context(nc.sbuf_tensor("sdu", [128, 4 * T], F32))
            sda = _es10.enter_context(nc.sbuf_tensor("sda", [128, T], F32))
            sdb = _es10.enter_context(nc.sbuf_tensor("sdb", [128, T], F32))
            sS = _es10.enter_context(nc.sbuf_tensor("sS", [128, T], F32))
            sgt = _es10.enter_context(nc.sbuf_tensor("sgt", [128, T], F32))
            sygb = _es10.enter_context(nc.sbuf_tensor("sygb", [128, T], BF16))
            ps_r1 = _es10.enter_context(nc.psum_tensor("ps_r1", [128, T], F32))
            ps_r2 = _es10.enter_context(nc.psum_tensor("ps_r2", [128, T], F32))
            if True:
                dma(s_apar[:], a_par)
                dma(s_dpar[:], d_par)
                for blk in range(NOT // 4):
                    for j in range(4):
                        dt_i = blk * 4 + j
                        dma(sdel[:, j * T:(j + 1) * T],
                            delta_f[dt_i * 128:(dt_i + 1) * 128, :])
                        dma(sy[:, j * T:(j + 1) * T],
                            u_f[dt_i * 128:(dt_i + 1) * 128, :])
                    for j in range(4):
                        dt_i = blk * 4 + j
                        vtt(sdu[:, j * T:(j + 1) * T], sdel[:, j * T:(j + 1) * T],
                            sy[:, j * T:(j + 1) * T], ALU.mult)
                        vts(out=sy[:, j * T:(j + 1) * T], in0=sy[:, j * T:(j + 1) * T],
                            scalar1=s_dpar[:, dt_i:dt_i + 1], scalar2=None, op0=ALU.mult)
                    for n in range(NS):
                        dma(btrow[:], bt_d[n:n + 1, :])
                        dma(ctrow[:], ct_d[n:n + 1, :])
                        wt(nc.tensor)
                        for s in range(NTB):
                            nc.tensor.matmul(out=ps_r1[:, s * 512:(s + 1) * 512],
                                             lhsT=s_ones[:],
                                             rhs=btrow[0:1, s * 512:(s + 1) * 512],
                                             start=True, stop=True)
                            inst = nc.tensor.matmul(out=ps_r2[:, s * 512:(s + 1) * 512],
                                                    lhsT=s_ones[:],
                                                    rhs=ctrow[0:1, s * 512:(s + 1) * 512],
                                                    start=True, stop=True)
                        fin(inst)
                        for j in range(4):
                            dt_i = blk * 4 + j
                            sca(out=sda[:], in_=sdel[:, j * T:(j + 1) * T], func=AF.Exp,
                                scale=s_apar[:, dt_i * NS + n:dt_i * NS + n + 1])
                            vtt(sdb[:], sdu[:, j * T:(j + 1) * T], ps_r1[:], ALU.mult)
                            wt(nc.vector)
                            fin(nc.vector.tensor_tensor_scan(out=sS[:], data0=sda[:],
                                                             data1=sdb[:], initial=0.0,
                                                             op0=ALU.mult, op1=ALU.add))
                            vtt(sdb[:], sS[:], ps_r2[:], ALU.mult)
                            vtt(sy[:, j * T:(j + 1) * T], sy[:, j * T:(j + 1) * T],
                                sdb[:], ALU.add)
                    for j in range(4):
                        dt_i = blk * 4 + j
                        dma(sgt[:], sg_f[dt_i * 128:(dt_i + 1) * 128, :])
                        vtt(sygb[:], sy[:, j * T:(j + 1) * T], sgt[:], ALU.mult)
                        dma(yg_td[:, dt_i].rearrange("t p q -> p t q"),
                            sygb[:].rearrange("p (t q) -> p t q", t=NTT))

            _es10.close()
            _es11 = ExitStack()
            swo = _es11.enter_context(nc.sbuf_tensor("swo", [128, NOT * 512], BF16))
            syg2 = _es11.enter_context(nc.sbuf_tensor("syg2", [128, NOT * 128], BF16))
            sxr = _es11.enter_context(nc.sbuf_tensor("sxr", [128, 512], F32))
            sx2 = _es11.enter_context(nc.sbuf_tensor("sx2", [128, 512], F32))
            ps_o = _es11.enter_context(nc.psum_tensor("ps_o", [128, 512], F32))
            if True:
                for hg in range(8):
                    dma(swo[:].rearrange("p (a h) -> p a h", a=NOT),
                        wout_bf[:, hg * 512:(hg + 1) * 512]
                        .rearrange("(a p) h -> p a h", p=128))
                    swov = swo.rearrange("p (a h) -> p a h", a=NOT)
                    for tt in range(NTT):
                        dma(syg2[:].rearrange("p (a q) -> p a q", a=NOT),
                            yg_td[tt].rearrange("a p q -> p a q"))
                        sygv2 = syg2.rearrange("p (a q) -> p a q", a=NOT)
                        wt(nc.tensor)
                        for dk in range(NOT):
                            inst = nc.tensor.matmul(out=ps_o[:], lhsT=sygv2[:, dk, :],
                                                    rhs=swov[:, dk, :],
                                                    start=(dk == 0), stop=(dk == NOT - 1))
                        fin(inst)
                        dma(sxr[:], x[tt * 128:(tt + 1) * 128, hg * 512:(hg + 1) * 512])
                        vtt(sx2[:], ps_o[:], sxr[:], ALU.add)
                        dma(x2_f[tt * 128:(tt + 1) * 128, hg * 512:(hg + 1) * 512], sx2[:])

            _es11.close()
            with nc.sbuf_tensor("ln2T", [128, NKT * T], BF16) as ln2T:
                _es12 = ExitStack()
                p1x2 = _es12.enter_context(nc.sbuf_tensor("p1x2", [128, H], F32))
                p1sq2 = _es12.enter_context(nc.sbuf_tensor("p1sq2", [128, 2048], F32))
                p1n2 = _es12.enter_context(nc.sbuf_tensor("p1n2", [128, H], BF16))
                ps_bf3 = _es12.enter_context(nc.psum_tensor("ps_bf3", [128, 512], BF16))
                if True:
                    layernorm_transpose(x2_f, ln2T, p1x2, p1sq2, p1n2, ps_bf3)

                _es12.close()
                _es13 = ExitStack()
                shwT = _es13.enter_context(nc.sbuf_tensor("shwT", [128, NTT * 128], BF16))
                sbc = _es13.enter_context(nc.sbuf_tensor("sbc", [128, H], BF16))
                sbb = _es13.enter_context(nc.sbuf_tensor("sbb", [1, H], F32))
                if True:
                    dma(sbc[:], bcomb_bf)
                    dma(sbb[:], b_base)
                    lv2 = ln2T.rearrange("p (k t) -> p k t", k=NKT)

                    _es14 = ExitStack()
                    swr = _es14.enter_context(nc.sbuf_tensor("swr", [128, NKT * 8], BF16))
                    sac = _es14.enter_context(nc.sbuf_tensor("sac", [128, NKT * 128], BF16))
                    sl = _es14.enter_context(nc.sbuf_tensor("sl", [128, 8], F32))
                    sl2 = _es14.enter_context(nc.sbuf_tensor("sl2", [128, 8], F32))
                    seq1 = _es14.enter_context(nc.sbuf_tensor("seq1", [128, 8], F32))
                    seq2 = _es14.enter_context(nc.sbuf_tensor("seq2", [128, 8], F32))
                    swm = _es14.enter_context(nc.sbuf_tensor("swm", [128, 8], F32))
                    shw = _es14.enter_context(nc.sbuf_tensor("shw", [128, 128], BF16))
                    ps_l = _es14.enter_context(nc.psum_tensor("ps_l", [128, 512], F32))
                    ps_h = _es14.enter_context(nc.psum_tensor("ps_h", [128, 128], F32))
                    ps_bf4 = _es14.enter_context(nc.psum_tensor("ps_bf4", [128, 128], BF16))
                    if True:
                        dma(swr[:].rearrange("p (k e) -> p k e", k=NKT),
                            wrouter_bf.rearrange("(k p) e -> p k e", p=128))
                        dma(sac[:].rearrange("p (k e) -> p k e", k=NKT),
                            acomb_bf.rearrange("(k p) e -> p k e", p=128))
                        swrv = swr.rearrange("p (k e) -> p k e", k=NKT)
                        sacv = sac.rearrange("p (k e) -> p k e", k=NKT)
                        for tt in range(NTT):
                            wt(nc.tensor)
                            for k in range(NKT):
                                inst = nc.tensor.matmul(
                                    out=ps_l[:, 0:8],
                                    lhsT=lv2[:, k, tt * 128:(tt + 1) * 128],
                                    rhs=swrv[:, k, :],
                                    start=(k == 0), stop=(k == NKT - 1))
                            fin(inst)
                            scopy(sl[:], ps_l[:, 0:8])
                            wt(nc.vector)
                            fin(nc.vector.tensor_reduce(out=col1[:], in_=sl[:], axis=AX.X,
                                                        op=ALU.max))
                            vts(out=seq1[:], in0=sl[:], scalar1=col1[:, 0:1], scalar2=None,
                                op0=ALU.is_ge)
                            vstt(out=sl2[:], in0=seq1[:], scalar=-1e30, in1=sl[:],
                                 op0=ALU.mult, op1=ALU.add)
                            wt(nc.vector)
                            fin(nc.vector.tensor_reduce(out=col2[:], in_=sl2[:], axis=AX.X,
                                                        op=ALU.max))
                            vts(out=seq2[:], in0=sl2[:], scalar1=col2[:, 0:1], scalar2=None,
                                op0=ALU.is_ge)
                            vtt(col3[:], col2[:], col1[:], ALU.subtract)
                            sca(out=col3[:], in_=col3[:], func=AF.Exp)
                            vts(out=col3[:], in0=col3[:], scalar1=1.0, scalar2=None,
                                op0=ALU.add)
                            wt(nc.vector)
                            fin(nc.vector.reciprocal(out=col3[:], in_=col3[:]))
                            vts(out=col4[:], in0=col3[:], scalar1=-1.0, scalar2=1.0,
                                op0=ALU.mult, op1=ALU.add)
                            vts(out=swm[:], in0=seq1[:], scalar1=col3[:, 0:1], scalar2=None,
                                op0=ALU.mult)
                            vstt(out=swm[:], in0=seq2[:], scalar=col4[:, 0:1], in1=swm[:],
                                 op0=ALU.mult, op1=ALU.add)
                            vts(out=swm[:], in0=swm[:], scalar1=2.0, scalar2=None,
                                op0=ALU.mult)
                            wt(nc.tensor)
                            for k in range(NKT):
                                inst = nc.tensor.matmul(
                                    out=ps_h[:],
                                    lhsT=lv2[:, k, tt * 128:(tt + 1) * 128],
                                    rhs=sacv[:, k, :],
                                    start=(k == 0), stop=(k == NKT - 1))
                            fin(inst)
                            for e in range(8):
                                vts(out=shw[:, e * 16:(e + 1) * 16],
                                    in0=ps_h[:, e * 16:(e + 1) * 16],
                                    scalar1=swm[:, e:e + 1], scalar2=None, op0=ALU.mult)
                            wt(nc.tensor)
                            fin(nc.tensor.transpose(out=ps_bf4[:], in_=shw[:],
                                                    identity=s_idb[:]))
                            scopy(shwT[:, tt * 128:(tt + 1) * 128], ps_bf4[:])

                    _es14.close()
                    _es15 = ExitStack()
                    swb = _es15.enter_context(nc.sbuf_tensor("swb", [128, NKT * 512], BF16))
                    sbrep = _es15.enter_context(nc.sbuf_tensor("sbrep", [128, 512], F32))
                    sff1 = _es15.enter_context(nc.sbuf_tensor("sff1", [128, 512], F32))
                    sxf = _es15.enter_context(nc.sbuf_tensor("sxf", [128, 512], F32))
                    ps_f = _es15.enter_context(nc.psum_tensor("ps_f", [128, 512], F32))
                    ps_q = _es15.enter_context(nc.psum_tensor("ps_q", [128, 512], F32))
                    if True:
                        for hg in range(8):
                            dma(swb[:].rearrange("p (k h) -> p k h", k=NKT),
                                wbase_bf[:, hg * 512:(hg + 1) * 512]
                                .rearrange("(k p) h -> p k h", p=128))
                            swbv = swb.rearrange("p (k h) -> p k h", k=NKT)
                            wt(nc.tensor)
                            fin(nc.tensor.matmul(out=ps_q[:], lhsT=s_ones[:],
                                                 rhs=sbb[0:1, hg * 512:(hg + 1) * 512],
                                                 start=True, stop=True))
                            scopy(sbrep[:], ps_q[:])
                            for tt in range(NTT):
                                wt(nc.tensor)
                                for k in range(NKT):
                                    nc.tensor.matmul(
                                        out=ps_f[:],
                                        lhsT=lv2[:, k, tt * 128:(tt + 1) * 128],
                                        rhs=swbv[:, k, :],
                                        start=(k == 0), stop=False)
                                inst = nc.tensor.matmul(
                                    out=ps_f[:],
                                    lhsT=shwT[:, tt * 128:(tt + 1) * 128],
                                    rhs=sbc[:, hg * 512:(hg + 1) * 512],
                                    start=False, stop=True)
                                fin(inst)
                                vtt(sff1[:], ps_f[:], sbrep[:], ALU.add)
                                dma(sxf[:], x2_f[tt * 128:(tt + 1) * 128,
                                                 hg * 512:(hg + 1) * 512])
                                vtt(sff1[:], sff1[:], sxf[:], ALU.add)
                                dma(y_out[tt * 128:(tt + 1) * 128,
                                          hg * 512:(hg + 1) * 512], sff1[:])
                    _es15.close()
                _es13.close()
            wt(nc.sync)
        _es3.close()
    return nc


def kernel(**inputs):
    hs = np.asarray(inputs["L0_attn_hidden_states"], np.float32)
    W_in = np.asarray(inputs["L0_attn_W_in"], np.float32)
    conv_w = np.asarray(inputs["L0_attn_conv_w"], np.float32)
    conv_b = np.asarray(inputs["L0_attn_conv_b"], np.float32)
    Wx = np.asarray(inputs["L0_attn_Wx"], np.float32)
    dt_ln_w = np.asarray(inputs["L0_attn_dt_ln_w"], np.float32)
    b_ln_w = np.asarray(inputs["L0_attn_b_ln_w"], np.float32)
    c_ln_w = np.asarray(inputs["L0_attn_c_ln_w"], np.float32)
    Wdt = np.asarray(inputs["L0_attn_Wdt"], np.float32)
    dt_bias = np.asarray(inputs["L0_attn_dt_bias"], np.float32)
    A = np.asarray(inputs["L0_attn_A"], np.float32)
    D_param = np.asarray(inputs["L0_attn_D_param"], np.float32)
    W_out = np.asarray(inputs["L0_attn_W_out"], np.float32)
    W_base = np.asarray(inputs["L0_ffn_W_base"], np.float32)
    b_base = np.asarray(inputs["L0_ffn_b_base"], np.float32)
    W_router = np.asarray(inputs["L0_ffn_W_router"], np.float32)
    A_ffn = np.asarray(inputs["L0_ffn_A"], np.float32)
    B_ffn = np.asarray(inputs["L0_ffn_B"], np.float32)

    wu_t = np.ascontiguousarray(
        W_in[:D].reshape(NOT, 128, NKT, 128).transpose(0, 2, 3, 1))
    wg_t = np.ascontiguousarray(
        W_in[D:].reshape(NOT, 128, NKT, 128).transpose(0, 2, 3, 1))
    wx_T = np.ascontiguousarray(Wx.T)
    wdt_t = np.ascontiguousarray(
        Wdt.T.reshape(2, 128, NOT, 128).transpose(0, 2, 1, 3))
    wout_T = np.ascontiguousarray(W_out.T)
    wbase_T = np.ascontiguousarray(W_base.T)
    wrouter_T = np.ascontiguousarray(W_router.T)
    acomb_T = np.ascontiguousarray(A_ffn.transpose(2, 0, 1).reshape(H, 128))
    bcomb_T = np.ascontiguousarray(B_ffn.transpose(0, 2, 1).reshape(128, H))
    conv_w_h = np.ascontiguousarray(conv_w.reshape(NOT, 128, 4).transpose(1, 0, 2)
                                    .reshape(128, NOT * 4))
    conv_b_h = np.ascontiguousarray(conv_b.reshape(NOT, 128).T)
    dt_bias_h = np.ascontiguousarray(dt_bias.reshape(NOT, 128).T)
    d_par_h = np.ascontiguousarray(D_param.reshape(NOT, 128).T)
    a_par_h = np.ascontiguousarray(A.reshape(NOT, 128, NS).transpose(1, 0, 2)
                                   .reshape(128, NOT * NS))
    dt_ln_h = np.ascontiguousarray(dt_ln_w.reshape(2, 128).T)
    b_ln_h = np.ascontiguousarray(b_ln_w.reshape(16, 1))
    c_ln_h = np.ascontiguousarray(c_ln_w.reshape(16, 1))
    ident_f = np.eye(128, dtype=np.float32)
    ident_b = np.eye(128).astype(ml_dtypes.bfloat16)
    ones_f = np.ones((1, 128), np.float32)
    cvals_h = np.zeros((128, 16), np.float32)
    cvals_h[:, :len(CONST_VALS)] = np.asarray(CONST_VALS, np.float32)[None, :]
    b_base_h = b_base.reshape(1, H)

    shared = dict(wu_t=wu_t, wg_t=wg_t, wx_T=wx_T, wdt_t=wdt_t, wout_T=wout_T,
                  wbase_T=wbase_T, wrouter_T=wrouter_T, acomb_T=acomb_T,
                  bcomb_T=bcomb_T, b_base=b_base_h, conv_w=conv_w_h,
                  conv_b=conv_b_h, dt_bias=dt_bias_h, d_par=d_par_h,
                  a_par=a_par_h, dt_ln_w=dt_ln_h, b_ln_w=b_ln_h, c_ln_w=c_ln_h,
                  ident_f=ident_f, ident_b=ident_b, ones_f=ones_f, cvals=cvals_h)

    nc = build_nc()
    in_maps = []
    for core in range(8):
        b = core if core < 4 else core - 4
        m = dict(shared)
        m["x"] = np.ascontiguousarray(hs[b])
        in_maps.append(m)

    res = run_bass_kernel_spmd(nc, in_maps, core_ids=list(range(8)))
    out = np.stack([np.asarray(res.results[b]["y_out"], np.float32)
                    for b in range(4)], axis=0)
    return out


# version 03
# speedup vs baseline: 1.0405x; 1.0405x over previous
import numpy as np
import ml_dtypes
from contextlib import ExitStack
import concourse.bass as bass
import concourse.mybir as mybir
from concourse.bass_utils import run_bass_kernel_spmd

F32 = mybir.dt.float32
BF16 = mybir.dt.bfloat16
AF = mybir.ActivationFunctionType
ALU = mybir.AluOpType
AX = mybir.AxisListType

H = 4096
D = 8192
NS = 16
R = 256
T = 2048
LN_EPS = 1e-5
RMS_EPS = 1e-6

CONST_VALS = [LN_EPS, RMS_EPS, 1.0, -0.5, -1.0, 1.0 / 4096, 1.0 / 256, 1.0 / 16]

NKT = H // 128
NOT = D // 128
NTT = T // 128
NTB = T // 512


def build_nc():
    nc = bass.Bass("TRN2", target_bir_lowering=False)

    x = nc.dram_tensor("x", [T, H], F32, kind="ExternalInput").ap()
    wu_t = nc.dram_tensor("wu_t", [NOT, NKT, 128, 128], F32, kind="ExternalInput").ap()
    wg_t = nc.dram_tensor("wg_t", [NOT, NKT, 128, 128], F32, kind="ExternalInput").ap()
    wx_T = nc.dram_tensor("wx_T", [D, 288], F32, kind="ExternalInput").ap()
    wdt_t = nc.dram_tensor("wdt_t", [2, NOT, 128, 128], F32, kind="ExternalInput").ap()
    wout_T = nc.dram_tensor("wout_T", [D, H], F32, kind="ExternalInput").ap()
    wbase_T = nc.dram_tensor("wbase_T", [H, H], F32, kind="ExternalInput").ap()
    wrouter_T = nc.dram_tensor("wrouter_T", [H, 8], F32, kind="ExternalInput").ap()
    acomb_T = nc.dram_tensor("acomb_T", [H, 128], F32, kind="ExternalInput").ap()
    bcomb_T = nc.dram_tensor("bcomb_T", [128, H], F32, kind="ExternalInput").ap()
    b_base = nc.dram_tensor("b_base", [1, H], F32, kind="ExternalInput").ap()
    conv_w = nc.dram_tensor("conv_w", [128, NOT * 4], F32, kind="ExternalInput").ap()
    conv_b = nc.dram_tensor("conv_b", [128, NOT], F32, kind="ExternalInput").ap()
    dt_bias = nc.dram_tensor("dt_bias", [128, NOT], F32, kind="ExternalInput").ap()
    d_par = nc.dram_tensor("d_par", [128, NOT], F32, kind="ExternalInput").ap()
    a_par = nc.dram_tensor("a_par", [128, NOT * NS], F32, kind="ExternalInput").ap()
    dt_ln_w = nc.dram_tensor("dt_ln_w", [128, 2], F32, kind="ExternalInput").ap()
    b_ln_w = nc.dram_tensor("b_ln_w", [16, 1], F32, kind="ExternalInput").ap()
    c_ln_w = nc.dram_tensor("c_ln_w", [16, 1], F32, kind="ExternalInput").ap()
    ident_f = nc.dram_tensor("ident_f", [128, 128], F32, kind="ExternalInput").ap()
    ident_b = nc.dram_tensor("ident_b", [128, 128], BF16, kind="ExternalInput").ap()
    ones_f = nc.dram_tensor("ones_f", [1, 128], F32, kind="ExternalInput").ap()
    cvals = nc.dram_tensor("cvals", [128, 16], F32, kind="ExternalInput").ap()

    y_out = nc.dram_tensor("y_out", [T, H], F32, kind="ExternalOutput").ap()

    wu_bf = nc.dram_tensor("wu_bf", [NOT, NKT, 128, 128], BF16, kind="Internal").ap()
    wg_bf = nc.dram_tensor("wg_bf", [NOT, NKT, 128, 128], BF16, kind="Internal").ap()
    wx_bf = nc.dram_tensor("wx_bf", [D, 288], BF16, kind="Internal").ap()
    wdt_bf = nc.dram_tensor("wdt_bf", [2, NOT, 128, 128], BF16, kind="Internal").ap()
    wout_bf = nc.dram_tensor("wout_bf", [D, H], BF16, kind="Internal").ap()
    wbase_bf = nc.dram_tensor("wbase_bf", [H, H], BF16, kind="Internal").ap()
    wrouter_bf = nc.dram_tensor("wrouter_bf", [H, 8], BF16, kind="Internal").ap()
    acomb_bf = nc.dram_tensor("acomb_bf", [H, 128], BF16, kind="Internal").ap()
    bcomb_bf = nc.dram_tensor("bcomb_bf", [128, H], BF16, kind="Internal").ap()
    u_f = nc.dram_tensor("u_f", [D, T], F32, kind="Internal").ap()
    ub_t = nc.dram_tensor("ub_t", [NTT, NOT, 128, 128], BF16, kind="Internal").ap()
    sg_f = nc.dram_tensor("sg_f", [D, T], F32, kind="Internal").ap()
    delta_f = nc.dram_tensor("delta_f", [D, T], F32, kind="Internal").ap()
    yg_td = nc.dram_tensor("yg_td", [NTT, NOT, 128, 128], BF16, kind="Internal").ap()
    x2_f = nc.dram_tensor("x2_f", [T, H], F32, kind="Internal").ap()
    bt_d = nc.dram_tensor("bt_d", [16, T], F32, kind="Internal").ap()
    ct_d = nc.dram_tensor("ct_d", [16, T], F32, kind="Internal").ap()

    G = [0]
    LAST = [None]

    with nc.semaphore("gs") as gs:

        def fin(inst, k=1):
            inst.then_inc(gs, k)
            G[0] += k

        def wt(eng):
            if G[0] > 0:
                eng.wait_ge(gs, G[0])
            LAST[0] = eng

        def dma(out, in_):
            if G[0] > 0:
                nc.sync.wait_ge(gs, G[0])
            LAST[0] = None
            fin(nc.sync.dma_start(out=out, in_=in_), 16)

        def sca(**kw):
            wt(nc.scalar)
            fin(nc.scalar.activation(**kw))

        def scopy(out, in_):
            wt(nc.scalar)
            fin(nc.scalar.copy(out=out, in_=in_))

        def vts(**kw):
            wt(nc.vector)
            fin(nc.vector.tensor_scalar(**kw))

        def vtt(out, in0, in1, op):
            wt(nc.vector)
            fin(nc.vector.tensor_tensor(out=out, in0=in0, in1=in1, op=op))

        def vstt(**kw):
            wt(nc.vector)
            fin(nc.vector.scalar_tensor_tensor(**kw))

        _es1 = ExitStack()
        c_in = _es1.enter_context(nc.sbuf_tensor("c_in", [128, 8192], F32))
        c_out = _es1.enter_context(nc.sbuf_tensor("c_out", [128, 8192], BF16))
        if True:
            def cast_flat(src, dst, nelem, pat):
                CH = 128 * 8192
                nfull = nelem // CH
                sv = src.rearrange(pat)
                dv = dst.rearrange(pat)
                for c in range(nfull):
                    s3 = sv[c * CH:(c + 1) * CH].rearrange("(p f) -> p f", p=128)
                    d3 = dv[c * CH:(c + 1) * CH].rearrange("(p f) -> p f", p=128)
                    dma(c_in[:], s3)
                    scopy(c_out[:], c_in[:])
                    dma(d3, c_out[:])
                rem = nelem - nfull * CH
                if rem:
                    fr = rem // 128
                    s3 = sv[nfull * CH:].rearrange("(p f) -> p f", p=128)
                    d3 = dv[nfull * CH:].rearrange("(p f) -> p f", p=128)
                    dma(c_in[:, :fr], s3)
                    scopy(c_out[:, :fr], c_in[:, :fr])
                    dma(d3, c_out[:, :fr])

            p2 = "a b -> (a b)"
            cast_flat(wx_T, wx_bf, D * 288, p2)
            cast_flat(wbase_T, wbase_bf, H * H, p2)
            cast_flat(wrouter_T, wrouter_bf, H * 8, p2)
            cast_flat(acomb_T, acomb_bf, H * 128, p2)
            cast_flat(bcomb_T, bcomb_bf, 128 * H, p2)

        _es1.close()
        _es2 = ExitStack()
        wd_in = _es2.enter_context(nc.sbuf_tensor("wd_in", [128, 1024], F32))
        wd_out = _es2.enter_context(nc.sbuf_tensor("wd_out", [128, 1024], BF16))
        s_lnw = _es2.enter_context(nc.sbuf_tensor("s_lnw", [128, 2], F32))
        if True:
            dma(s_lnw[:], dt_ln_w)
            for rk in range(2):
                for g in range(NOT // 8):
                    src = wdt_t[rk, g * 8:(g + 1) * 8].rearrange("t p d -> p t d")
                    dst = wdt_bf[rk, g * 8:(g + 1) * 8].rearrange("t p d -> p t d")
                    dma(wd_in[:].rearrange("p (t d) -> p t d", t=8), src)
                    vts(out=wd_out[:], in0=wd_in[:], scalar1=s_lnw[:, rk:rk + 1],
                        scalar2=None, op0=ALU.mult)
                    dma(dst, wd_out[:].rearrange("p (t d) -> p t d", t=8))

        _es2.close()
        _es3 = ExitStack()
        s_idf = _es3.enter_context(nc.sbuf_tensor("s_idf", [128, 128], F32))
        s_idb = _es3.enter_context(nc.sbuf_tensor("s_idb", [128, 128], BF16))
        s_ones = _es3.enter_context(nc.sbuf_tensor("s_ones", [1, 128], F32))
        col1 = _es3.enter_context(nc.sbuf_tensor("col1", [128, 1], F32))
        col2 = _es3.enter_context(nc.sbuf_tensor("col2", [128, 1], F32))
        col3 = _es3.enter_context(nc.sbuf_tensor("col3", [128, 1], F32))
        col4 = _es3.enter_context(nc.sbuf_tensor("col4", [128, 1], F32))
        s_cv = _es3.enter_context(nc.sbuf_tensor("s_cv", [128, 16], F32))
        if True:
            dma(s_idf[:], ident_f)
            dma(s_idb[:], ident_b)
            dma(s_ones[:], ones_f)
            dma(s_cv[:], cvals)
            for ci, cval in enumerate(CONST_VALS):
                nc.const_aps.register(F32, cval, s_cv[:, ci:ci + 1])

            def layernorm_transpose(src_rows, dstT, p1x, p1sq, p1n, ps_bf):
                dstv = dstT.rearrange("p (k t) -> p k t", k=NKT)
                for tt in range(NTT):
                    dma(p1x[:], src_rows[tt * 128:(tt + 1) * 128, :])
                    wt(nc.vector)
                    fin(nc.vector.tensor_reduce(out=col1[:], in_=p1x[:], axis=AX.X, op=ALU.add))
                    wt(nc.scalar)
                    fin(nc.scalar.mul(col1[:], col1[:], 1.0 / H))
                    sca(out=p1sq[:], in_=p1x[:, 0:2048], func=AF.Square, accum_out=col2[:])
                    sca(out=p1sq[:], in_=p1x[:, 2048:4096], func=AF.Square, accum_out=col4[:])
                    vtt(col2[:], col2[:], col4[:], ALU.add)
                    vts(out=col3[:], in0=col1[:], scalar1=col1[:, 0:1], scalar2=None,
                        op0=ALU.mult)
                    vstt(out=col2[:], in0=col2[:], scalar=1.0 / H, in1=col3[:],
                         op0=ALU.mult, op1=ALU.subtract)
                    sca(out=col2[:], in_=col2[:], func=AF.Ln, bias=LN_EPS)
                    sca(out=col2[:], in_=col2[:], func=AF.Exp, scale=-0.5)
                    vstt(out=p1n[:], in0=p1x[:], scalar=col1[:, 0:1],
                         in1=col2[:, 0:1].broadcast_to([128, H]),
                         op0=ALU.subtract, op1=ALU.mult)
                    for hq in range(8):
                        wt(nc.tensor)
                        for j in range(4):
                            hb = hq * 4 + j
                            inst = nc.tensor.transpose(out=ps_bf[:, j * 128:(j + 1) * 128],
                                                       in_=p1n[:, hb * 128:(hb + 1) * 128],
                                                       identity=s_idb[:])
                        fin(inst)
                        scopy(dstv[:, hq * 4:(hq + 1) * 4, tt * 128:(tt + 1) * 128],
                              ps_bf[:].rearrange("p (j q) -> p j q", j=4))

            with nc.sbuf_tensor("lnxT", [128, NKT * T], BF16) as lnxT:
                _es4 = ExitStack()
                p1x = _es4.enter_context(nc.sbuf_tensor("p1x", [128, H], F32))
                p1sq = _es4.enter_context(nc.sbuf_tensor("p1sq", [128, 2048], F32))
                p1n = _es4.enter_context(nc.sbuf_tensor("p1n", [128, H], BF16))
                ps_bf1 = _es4.enter_context(nc.psum_tensor("ps_bf1", [128, 512], BF16))
                if True:
                    layernorm_transpose(x, lnxT, p1x, p1sq, p1n, ps_bf1)

                _es4.close()
                _es5 = ExitStack()
                s_convw = _es5.enter_context(nc.sbuf_tensor("s_convw", [128, NOT * 4], F32))
                s_convb = _es5.enter_context(nc.sbuf_tensor("s_convb", [128, NOT], F32))
                sw = _es5.enter_context(nc.sbuf_tensor("sw", [128, NKT * 128], BF16))
                sw_f = _es5.enter_context(nc.sbuf_tensor("sw_f", [128, NKT * 128], F32))
                raw = _es5.enter_context(nc.sbuf_tensor("raw", [128, T + 3], F32))
                su = _es5.enter_context(nc.sbuf_tensor("su", [128, T], F32))
                sub = _es5.enter_context(nc.sbuf_tensor("sub", [128, T], BF16))
                ps_u = _es5.enter_context(nc.psum_tensor("ps_u", [128, 512], F32))
                if True:
                    dma(s_convw[:], conv_w)
                    dma(s_convb[:], conv_b)
                    lv = lnxT.rearrange("p (k t) -> p k t", k=NKT)
                    for wsrc, do_conv in ((wu_t, True), (wg_t, False)):
                        for ot in range(NOT):
                            dma(sw_f[:].rearrange("p (k o) -> p k o", k=NKT),
                                wsrc[ot].rearrange("k p o -> p k o"))
                            scopy(sw[:], sw_f[:])
                            swv = sw.rearrange("p (k o) -> p k o", k=NKT)
                            for tb in range(NTB):
                                wt(nc.tensor)
                                for k in range(NKT):
                                    inst = nc.tensor.matmul(
                                        out=ps_u[:], lhsT=swv[:, k, :],
                                        rhs=lv[:, k, tb * 512:(tb + 1) * 512],
                                        start=(k == 0), stop=(k == NKT - 1))
                                fin(inst)
                                if do_conv:
                                    scopy(raw[:, 3 + tb * 512:3 + (tb + 1) * 512], ps_u[:])
                                else:
                                    sca(out=su[:, tb * 512:(tb + 1) * 512], in_=ps_u[:],
                                        func=AF.Silu)
                            if do_conv:
                                wt(nc.vector)
                                fin(nc.vector.memset(raw[:, 0:3], 0.0))
                                vts(out=su[:], in0=raw[:, 0:T],
                                    scalar1=s_convw[:, ot * 4:ot * 4 + 1],
                                    scalar2=None, op0=ALU.mult)
                                for kk in range(1, 4):
                                    vstt(out=su[:], in0=raw[:, kk:T + kk],
                                         scalar=s_convw[:, ot * 4 + kk:ot * 4 + kk + 1],
                                         in1=su[:], op0=ALU.mult, op1=ALU.add)
                                sca(out=su[:], in_=su[:], func=AF.Silu,
                                    bias=s_convb[:, ot:ot + 1])
                                scopy(sub[:], su[:])
                                dma(u_f[ot * 128:(ot + 1) * 128, :], su[:])
                                dma(ub_t[:, ot].rearrange("t p q -> p t q"),
                                    sub[:].rearrange("p (t q) -> p t q", t=NTT))
                            else:
                                dma(sg_f[ot * 128:(ot + 1) * 128, :], su[:])

                _es5.close()
            _es6 = ExitStack()
            ssm_sb = _es6.enter_context(nc.sbuf_tensor("ssm_sb", [128, NTT * 288], F32))
            dtnT_sb = _es6.enter_context(nc.sbuf_tensor("dtnT_sb", [128, 2 * T], BF16))
            BT_sb = _es6.enter_context(nc.sbuf_tensor("BT_sb", [16, T], F32))
            CT_sb = _es6.enter_context(nc.sbuf_tensor("CT_sb", [16, T], F32))
            if True:
                _es7 = ExitStack()
                syg = _es7.enter_context(nc.sbuf_tensor("syg", [128, NOT * 128], BF16))
                swxa = _es7.enter_context(nc.sbuf_tensor("swxa", [128, NOT * 288], BF16))
                ps_s = _es7.enter_context(nc.psum_tensor("ps_s", [128, 512], F32))
                if True:
                    dma(swxa[:].rearrange("p (a n) -> p a n", a=NOT),
                        wx_bf.rearrange("(a p) n -> p a n", p=128))
                    swxv = swxa.rearrange("p (a n) -> p a n", a=NOT)
                    for tt in range(NTT):
                        dma(syg[:].rearrange("p (a q) -> p a q", a=NOT),
                            ub_t[tt].rearrange("a p q -> p a q"))
                        sygv = syg.rearrange("p (a q) -> p a q", a=NOT)
                        wt(nc.tensor)
                        for dk in range(NOT):
                            inst = nc.tensor.matmul(out=ps_s[:, 0:288], lhsT=sygv[:, dk, :],
                                                    rhs=swxv[:, dk, :],
                                                    start=(dk == 0), stop=(dk == NOT - 1))
                        fin(inst)
                        scopy(ssm_sb[:, tt * 288:(tt + 1) * 288], ps_s[:, 0:288])

                _es7.close()
                _es8 = ExitStack()
                s_blnw = _es8.enter_context(nc.sbuf_tensor("s_blnw", [16, 1], F32))
                s_clnw = _es8.enter_context(nc.sbuf_tensor("s_clnw", [16, 1], F32))
                p4s = _es8.enter_context(nc.sbuf_tensor("p4s", [128, 256], F32))
                p4b = _es8.enter_context(nc.sbuf_tensor("p4b", [128, 256], BF16))
                p4f = _es8.enter_context(nc.sbuf_tensor("p4f", [128, 16], F32))
                ps_bf2 = _es8.enter_context(nc.psum_tensor("ps_bf2", [128, 128], BF16))
                ps_t32 = _es8.enter_context(nc.psum_tensor("ps_t32", [128, 128], F32))
                if True:
                    dma(s_blnw[:], b_ln_w)
                    dma(s_clnw[:], c_ln_w)
                    for tt in range(NTT):
                        base = ssm_sb[:, tt * 288:(tt + 1) * 288]
                        sca(out=p4s[:], in_=base[:, 0:256], func=AF.Square, accum_out=col1[:])
                        sca(out=col1[:], in_=col1[:], func=AF.Ln, scale=1.0 / R, bias=RMS_EPS)
                        sca(out=col1[:], in_=col1[:], func=AF.Exp, scale=-0.5)
                        vts(out=p4b[:], in0=base[:, 0:256], scalar1=col1[:, 0:1],
                            scalar2=None, op0=ALU.mult)
                        for rk in range(2):
                            wt(nc.tensor)
                            fin(nc.tensor.transpose(out=ps_bf2[:],
                                                    in_=p4b[:, rk * 128:(rk + 1) * 128],
                                                    identity=s_idb[:]))
                            scopy(dtnT_sb[:, rk * T + tt * 128:rk * T + (tt + 1) * 128],
                                  ps_bf2[:])
                        for off, dstT in ((256, BT_sb), (272, CT_sb)):
                            sca(out=p4f[:], in_=base[:, off:off + 16], func=AF.Square,
                                accum_out=col1[:])
                            sca(out=col1[:], in_=col1[:], func=AF.Ln, scale=1.0 / NS,
                                bias=RMS_EPS)
                            sca(out=col1[:], in_=col1[:], func=AF.Exp, scale=-0.5)
                            vts(out=p4f[:], in0=base[:, off:off + 16], scalar1=col1[:, 0:1],
                                scalar2=None, op0=ALU.mult)
                            wt(nc.tensor)
                            fin(nc.tensor.transpose(out=ps_t32[0:16, :], in_=p4f[:],
                                                    identity=s_idf[:]))
                            scopy(dstT[:, tt * 128:(tt + 1) * 128], ps_t32[0:16, :])
                    vts(out=BT_sb[:], in0=BT_sb[:], scalar1=s_blnw[:, 0:1], scalar2=None,
                        op0=ALU.mult)
                    vts(out=CT_sb[:], in0=CT_sb[:], scalar1=s_clnw[:, 0:1], scalar2=None,
                        op0=ALU.mult)
                    dma(bt_d, BT_sb[:])
                    dma(ct_d, CT_sb[:])

                _es8.close()
                _es9 = ExitStack()
                s_dtb = _es9.enter_context(nc.sbuf_tensor("s_dtb", [128, NOT], F32))
                swdt = _es9.enter_context(nc.sbuf_tensor("swdt", [128, 2 * NOT * 128], BF16))
                d1 = _es9.enter_context(nc.sbuf_tensor("d1", [128, 512], F32))
                d2 = _es9.enter_context(nc.sbuf_tensor("d2", [128, 512], F32))
                d3 = _es9.enter_context(nc.sbuf_tensor("d3", [128, 512], F32))
                ps_d = _es9.enter_context(nc.psum_tensor("ps_d", [128, 512], F32))
                if True:
                    dma(s_dtb[:], dt_bias)
                    dma(swdt[:].rearrange("p (r a d) -> p r a d", r=2, a=NOT),
                        wdt_bf.rearrange("r a p d -> p r a d"))
                    swdtv = swdt.rearrange("p (r a d) -> p r a d", r=2, a=NOT)
                    dtv = dtnT_sb.rearrange("p (r t) -> p r t", r=2)
                    for dc in range(NOT):
                        for tb in range(NTB):
                            wt(nc.tensor)
                            for rk in range(2):
                                inst = nc.tensor.matmul(out=ps_d[:], lhsT=swdtv[:, rk, dc, :],
                                                        rhs=dtv[:, rk, tb * 512:(tb + 1) * 512],
                                                        start=(rk == 0), stop=(rk == 1))
                            fin(inst)
                            bias = s_dtb[:, dc:dc + 1]
                            sca(out=d1[:], in_=ps_d[:], func=AF.Abs, bias=bias)
                            sca(out=d1[:], in_=d1[:], func=AF.Exp, scale=-1.0)
                            sca(out=d1[:], in_=d1[:], func=AF.Ln, bias=1.0)
                            sca(out=d2[:], in_=ps_d[:], func=AF.Relu, bias=bias)
                            vtt(d3[:], d1[:], d2[:], ALU.add)
                            dma(delta_f[dc * 128:(dc + 1) * 128, tb * 512:(tb + 1) * 512],
                                d3[:])

                _es9.close()
            _es6.close()
            _es10 = ExitStack()
            s_apar = _es10.enter_context(nc.sbuf_tensor("s_apar", [128, NOT * NS], F32))
            s_dpar = _es10.enter_context(nc.sbuf_tensor("s_dpar", [128, NOT], F32))
            btrow = _es10.enter_context(nc.sbuf_tensor("btrow", [1, T], F32))
            ctrow = _es10.enter_context(nc.sbuf_tensor("ctrow", [1, T], F32))
            sdel = _es10.enter_context(nc.sbuf_tensor("sdel", [128, 4 * T], F32))
            sy = _es10.enter_context(nc.sbuf_tensor("sy", [128, 4 * T], F32))
            sdu = _es10.enter_context(nc.sbuf_tensor("sdu", [128, 4 * T], F32))
            sda = _es10.enter_context(nc.sbuf_tensor("sda", [128, T], F32))
            sdb = _es10.enter_context(nc.sbuf_tensor("sdb", [128, T], F32))
            sS = _es10.enter_context(nc.sbuf_tensor("sS", [128, T], F32))
            sgt = _es10.enter_context(nc.sbuf_tensor("sgt", [128, T], F32))
            sygb = _es10.enter_context(nc.sbuf_tensor("sygb", [128, T], BF16))
            ps_r1 = _es10.enter_context(nc.psum_tensor("ps_r1", [128, T], F32))
            ps_r2 = _es10.enter_context(nc.psum_tensor("ps_r2", [128, T], F32))
            if True:
                dma(s_apar[:], a_par)
                dma(s_dpar[:], d_par)
                for blk in range(NOT // 4):
                    for j in range(4):
                        dt_i = blk * 4 + j
                        dma(sdel[:, j * T:(j + 1) * T],
                            delta_f[dt_i * 128:(dt_i + 1) * 128, :])
                        dma(sy[:, j * T:(j + 1) * T],
                            u_f[dt_i * 128:(dt_i + 1) * 128, :])
                    for j in range(4):
                        dt_i = blk * 4 + j
                        vtt(sdu[:, j * T:(j + 1) * T], sdel[:, j * T:(j + 1) * T],
                            sy[:, j * T:(j + 1) * T], ALU.mult)
                        vts(out=sy[:, j * T:(j + 1) * T], in0=sy[:, j * T:(j + 1) * T],
                            scalar1=s_dpar[:, dt_i:dt_i + 1], scalar2=None, op0=ALU.mult)
                    for n in range(NS):
                        dma(btrow[:], bt_d[n:n + 1, :])
                        dma(ctrow[:], ct_d[n:n + 1, :])
                        wt(nc.tensor)
                        for s in range(NTB):
                            nc.tensor.matmul(out=ps_r1[:, s * 512:(s + 1) * 512],
                                             lhsT=s_ones[:],
                                             rhs=btrow[0:1, s * 512:(s + 1) * 512],
                                             start=True, stop=True)
                            inst = nc.tensor.matmul(out=ps_r2[:, s * 512:(s + 1) * 512],
                                                    lhsT=s_ones[:],
                                                    rhs=ctrow[0:1, s * 512:(s + 1) * 512],
                                                    start=True, stop=True)
                        fin(inst)
                        for j in range(4):
                            dt_i = blk * 4 + j
                            sca(out=sda[:], in_=sdel[:, j * T:(j + 1) * T], func=AF.Exp,
                                scale=s_apar[:, dt_i * NS + n:dt_i * NS + n + 1])
                            vtt(sdb[:], sdu[:, j * T:(j + 1) * T], ps_r1[:], ALU.mult)
                            wt(nc.vector)
                            fin(nc.vector.tensor_tensor_scan(out=sS[:], data0=sda[:],
                                                             data1=sdb[:], initial=0.0,
                                                             op0=ALU.mult, op1=ALU.add))
                            vtt(sdb[:], sS[:], ps_r2[:], ALU.mult)
                            vtt(sy[:, j * T:(j + 1) * T], sy[:, j * T:(j + 1) * T],
                                sdb[:], ALU.add)
                    for j in range(4):
                        dt_i = blk * 4 + j
                        dma(sgt[:], sg_f[dt_i * 128:(dt_i + 1) * 128, :])
                        vtt(sygb[:], sy[:, j * T:(j + 1) * T], sgt[:], ALU.mult)
                        dma(yg_td[:, dt_i].rearrange("t p q -> p t q"),
                            sygb[:].rearrange("p (t q) -> p t q", t=NTT))

            _es10.close()
            _es11 = ExitStack()
            swo = _es11.enter_context(nc.sbuf_tensor("swo", [128, NOT * 512], BF16))
            swo_f = _es11.enter_context(nc.sbuf_tensor("swo_f", [128, NOT * 256], F32))
            syg2 = _es11.enter_context(nc.sbuf_tensor("syg2", [128, NOT * 128], BF16))
            sxr = _es11.enter_context(nc.sbuf_tensor("sxr", [128, 512], F32))
            sx2 = _es11.enter_context(nc.sbuf_tensor("sx2", [128, 512], F32))
            ps_o = _es11.enter_context(nc.psum_tensor("ps_o", [128, 512], F32))
            if True:
                for hg in range(8):
                    for half in range(2):
                        dma(swo_f[:].rearrange("p (a h) -> p a h", a=NOT // 2),
                            wout_T[half * (D // 2):(half + 1) * (D // 2),
                                   hg * 512:(hg + 1) * 512]
                            .rearrange("(a p) h -> p a h", p=128))
                        scopy(swo[:, half * (NOT // 2) * 512:(half + 1) * (NOT // 2) * 512],
                              swo_f[:])
                    swov = swo.rearrange("p (a h) -> p a h", a=NOT)
                    for tt in range(NTT):
                        dma(syg2[:].rearrange("p (a q) -> p a q", a=NOT),
                            yg_td[tt].rearrange("a p q -> p a q"))
                        sygv2 = syg2.rearrange("p (a q) -> p a q", a=NOT)
                        wt(nc.tensor)
                        for dk in range(NOT):
                            inst = nc.tensor.matmul(out=ps_o[:], lhsT=sygv2[:, dk, :],
                                                    rhs=swov[:, dk, :],
                                                    start=(dk == 0), stop=(dk == NOT - 1))
                        fin(inst)
                        dma(sxr[:], x[tt * 128:(tt + 1) * 128, hg * 512:(hg + 1) * 512])
                        vtt(sx2[:], ps_o[:], sxr[:], ALU.add)
                        dma(x2_f[tt * 128:(tt + 1) * 128, hg * 512:(hg + 1) * 512], sx2[:])

            _es11.close()
            with nc.sbuf_tensor("ln2T", [128, NKT * T], BF16) as ln2T:
                _es12 = ExitStack()
                p1x2 = _es12.enter_context(nc.sbuf_tensor("p1x2", [128, H], F32))
                p1sq2 = _es12.enter_context(nc.sbuf_tensor("p1sq2", [128, 2048], F32))
                p1n2 = _es12.enter_context(nc.sbuf_tensor("p1n2", [128, H], BF16))
                ps_bf3 = _es12.enter_context(nc.psum_tensor("ps_bf3", [128, 512], BF16))
                if True:
                    layernorm_transpose(x2_f, ln2T, p1x2, p1sq2, p1n2, ps_bf3)

                _es12.close()
                _es13 = ExitStack()
                shwT = _es13.enter_context(nc.sbuf_tensor("shwT", [128, NTT * 128], BF16))
                sbc = _es13.enter_context(nc.sbuf_tensor("sbc", [128, H], BF16))
                sbb = _es13.enter_context(nc.sbuf_tensor("sbb", [1, H], F32))
                if True:
                    dma(sbc[:], bcomb_bf)
                    dma(sbb[:], b_base)
                    lv2 = ln2T.rearrange("p (k t) -> p k t", k=NKT)

                    _es14 = ExitStack()
                    swr = _es14.enter_context(nc.sbuf_tensor("swr", [128, NKT * 8], BF16))
                    sac = _es14.enter_context(nc.sbuf_tensor("sac", [128, NKT * 128], BF16))
                    sl = _es14.enter_context(nc.sbuf_tensor("sl", [128, 8], F32))
                    sl2 = _es14.enter_context(nc.sbuf_tensor("sl2", [128, 8], F32))
                    seq1 = _es14.enter_context(nc.sbuf_tensor("seq1", [128, 8], F32))
                    seq2 = _es14.enter_context(nc.sbuf_tensor("seq2", [128, 8], F32))
                    swm = _es14.enter_context(nc.sbuf_tensor("swm", [128, 8], F32))
                    shw = _es14.enter_context(nc.sbuf_tensor("shw", [128, 128], BF16))
                    ps_l = _es14.enter_context(nc.psum_tensor("ps_l", [128, 512], F32))
                    ps_h = _es14.enter_context(nc.psum_tensor("ps_h", [128, 128], F32))
                    ps_bf4 = _es14.enter_context(nc.psum_tensor("ps_bf4", [128, 128], BF16))
                    if True:
                        dma(swr[:].rearrange("p (k e) -> p k e", k=NKT),
                            wrouter_bf.rearrange("(k p) e -> p k e", p=128))
                        dma(sac[:].rearrange("p (k e) -> p k e", k=NKT),
                            acomb_bf.rearrange("(k p) e -> p k e", p=128))
                        swrv = swr.rearrange("p (k e) -> p k e", k=NKT)
                        sacv = sac.rearrange("p (k e) -> p k e", k=NKT)
                        for tt in range(NTT):
                            wt(nc.tensor)
                            for k in range(NKT):
                                inst = nc.tensor.matmul(
                                    out=ps_l[:, 0:8],
                                    lhsT=lv2[:, k, tt * 128:(tt + 1) * 128],
                                    rhs=swrv[:, k, :],
                                    start=(k == 0), stop=(k == NKT - 1))
                            fin(inst)
                            scopy(sl[:], ps_l[:, 0:8])
                            wt(nc.vector)
                            fin(nc.vector.tensor_reduce(out=col1[:], in_=sl[:], axis=AX.X,
                                                        op=ALU.max))
                            vts(out=seq1[:], in0=sl[:], scalar1=col1[:, 0:1], scalar2=None,
                                op0=ALU.is_ge)
                            vstt(out=sl2[:], in0=seq1[:], scalar=-1e30, in1=sl[:],
                                 op0=ALU.mult, op1=ALU.add)
                            wt(nc.vector)
                            fin(nc.vector.tensor_reduce(out=col2[:], in_=sl2[:], axis=AX.X,
                                                        op=ALU.max))
                            vts(out=seq2[:], in0=sl2[:], scalar1=col2[:, 0:1], scalar2=None,
                                op0=ALU.is_ge)
                            vtt(col3[:], col2[:], col1[:], ALU.subtract)
                            sca(out=col3[:], in_=col3[:], func=AF.Exp)
                            vts(out=col3[:], in0=col3[:], scalar1=1.0, scalar2=None,
                                op0=ALU.add)
                            wt(nc.vector)
                            fin(nc.vector.reciprocal(out=col3[:], in_=col3[:]))
                            vts(out=col4[:], in0=col3[:], scalar1=-1.0, scalar2=1.0,
                                op0=ALU.mult, op1=ALU.add)
                            vts(out=swm[:], in0=seq1[:], scalar1=col3[:, 0:1], scalar2=None,
                                op0=ALU.mult)
                            vstt(out=swm[:], in0=seq2[:], scalar=col4[:, 0:1], in1=swm[:],
                                 op0=ALU.mult, op1=ALU.add)
                            vts(out=swm[:], in0=swm[:], scalar1=2.0, scalar2=None,
                                op0=ALU.mult)
                            wt(nc.tensor)
                            for k in range(NKT):
                                inst = nc.tensor.matmul(
                                    out=ps_h[:],
                                    lhsT=lv2[:, k, tt * 128:(tt + 1) * 128],
                                    rhs=sacv[:, k, :],
                                    start=(k == 0), stop=(k == NKT - 1))
                            fin(inst)
                            for e in range(8):
                                vts(out=shw[:, e * 16:(e + 1) * 16],
                                    in0=ps_h[:, e * 16:(e + 1) * 16],
                                    scalar1=swm[:, e:e + 1], scalar2=None, op0=ALU.mult)
                            wt(nc.tensor)
                            fin(nc.tensor.transpose(out=ps_bf4[:], in_=shw[:],
                                                    identity=s_idb[:]))
                            scopy(shwT[:, tt * 128:(tt + 1) * 128], ps_bf4[:])

                    _es14.close()
                    _es15 = ExitStack()
                    swb = _es15.enter_context(nc.sbuf_tensor("swb", [128, NKT * 512], BF16))
                    sbrep = _es15.enter_context(nc.sbuf_tensor("sbrep", [128, 512], F32))
                    sff1 = _es15.enter_context(nc.sbuf_tensor("sff1", [128, 512], F32))
                    sxf = _es15.enter_context(nc.sbuf_tensor("sxf", [128, 512], F32))
                    ps_f = _es15.enter_context(nc.psum_tensor("ps_f", [128, 512], F32))
                    ps_q = _es15.enter_context(nc.psum_tensor("ps_q", [128, 512], F32))
                    if True:
                        for hg in range(8):
                            dma(swb[:].rearrange("p (k h) -> p k h", k=NKT),
                                wbase_bf[:, hg * 512:(hg + 1) * 512]
                                .rearrange("(k p) h -> p k h", p=128))
                            swbv = swb.rearrange("p (k h) -> p k h", k=NKT)
                            wt(nc.tensor)
                            fin(nc.tensor.matmul(out=ps_q[:], lhsT=s_ones[:],
                                                 rhs=sbb[0:1, hg * 512:(hg + 1) * 512],
                                                 start=True, stop=True))
                            scopy(sbrep[:], ps_q[:])
                            for tt in range(NTT):
                                wt(nc.tensor)
                                for k in range(NKT):
                                    nc.tensor.matmul(
                                        out=ps_f[:],
                                        lhsT=lv2[:, k, tt * 128:(tt + 1) * 128],
                                        rhs=swbv[:, k, :],
                                        start=(k == 0), stop=False)
                                inst = nc.tensor.matmul(
                                    out=ps_f[:],
                                    lhsT=shwT[:, tt * 128:(tt + 1) * 128],
                                    rhs=sbc[:, hg * 512:(hg + 1) * 512],
                                    start=False, stop=True)
                                fin(inst)
                                vtt(sff1[:], ps_f[:], sbrep[:], ALU.add)
                                dma(sxf[:], x2_f[tt * 128:(tt + 1) * 128,
                                                 hg * 512:(hg + 1) * 512])
                                vtt(sff1[:], sff1[:], sxf[:], ALU.add)
                                dma(y_out[tt * 128:(tt + 1) * 128,
                                          hg * 512:(hg + 1) * 512], sff1[:])
                    _es15.close()
                _es13.close()
            wt(nc.sync)
        _es3.close()
    return nc


def kernel(**inputs):
    hs = np.asarray(inputs["L0_attn_hidden_states"], np.float32)
    W_in = np.asarray(inputs["L0_attn_W_in"], np.float32)
    conv_w = np.asarray(inputs["L0_attn_conv_w"], np.float32)
    conv_b = np.asarray(inputs["L0_attn_conv_b"], np.float32)
    Wx = np.asarray(inputs["L0_attn_Wx"], np.float32)
    dt_ln_w = np.asarray(inputs["L0_attn_dt_ln_w"], np.float32)
    b_ln_w = np.asarray(inputs["L0_attn_b_ln_w"], np.float32)
    c_ln_w = np.asarray(inputs["L0_attn_c_ln_w"], np.float32)
    Wdt = np.asarray(inputs["L0_attn_Wdt"], np.float32)
    dt_bias = np.asarray(inputs["L0_attn_dt_bias"], np.float32)
    A = np.asarray(inputs["L0_attn_A"], np.float32)
    D_param = np.asarray(inputs["L0_attn_D_param"], np.float32)
    W_out = np.asarray(inputs["L0_attn_W_out"], np.float32)
    W_base = np.asarray(inputs["L0_ffn_W_base"], np.float32)
    b_base = np.asarray(inputs["L0_ffn_b_base"], np.float32)
    W_router = np.asarray(inputs["L0_ffn_W_router"], np.float32)
    A_ffn = np.asarray(inputs["L0_ffn_A"], np.float32)
    B_ffn = np.asarray(inputs["L0_ffn_B"], np.float32)

    wu_t = np.ascontiguousarray(
        W_in[:D].reshape(NOT, 128, NKT, 128).transpose(0, 2, 3, 1))
    wg_t = np.ascontiguousarray(
        W_in[D:].reshape(NOT, 128, NKT, 128).transpose(0, 2, 3, 1))
    wx_T = np.ascontiguousarray(Wx.T)
    wdt_t = np.ascontiguousarray(
        Wdt.T.reshape(2, 128, NOT, 128).transpose(0, 2, 1, 3))
    wout_T = np.ascontiguousarray(W_out.T)
    wbase_T = np.ascontiguousarray(W_base.T)
    wrouter_T = np.ascontiguousarray(W_router.T)
    acomb_T = np.ascontiguousarray(A_ffn.transpose(2, 0, 1).reshape(H, 128))
    bcomb_T = np.ascontiguousarray(B_ffn.transpose(0, 2, 1).reshape(128, H))
    conv_w_h = np.ascontiguousarray(conv_w.reshape(NOT, 128, 4).transpose(1, 0, 2)
                                    .reshape(128, NOT * 4))
    conv_b_h = np.ascontiguousarray(conv_b.reshape(NOT, 128).T)
    dt_bias_h = np.ascontiguousarray(dt_bias.reshape(NOT, 128).T)
    d_par_h = np.ascontiguousarray(D_param.reshape(NOT, 128).T)
    a_par_h = np.ascontiguousarray(A.reshape(NOT, 128, NS).transpose(1, 0, 2)
                                   .reshape(128, NOT * NS))
    dt_ln_h = np.ascontiguousarray(dt_ln_w.reshape(2, 128).T)
    b_ln_h = np.ascontiguousarray(b_ln_w.reshape(16, 1))
    c_ln_h = np.ascontiguousarray(c_ln_w.reshape(16, 1))
    ident_f = np.eye(128, dtype=np.float32)
    ident_b = np.eye(128).astype(ml_dtypes.bfloat16)
    ones_f = np.ones((1, 128), np.float32)
    cvals_h = np.zeros((128, 16), np.float32)
    cvals_h[:, :len(CONST_VALS)] = np.asarray(CONST_VALS, np.float32)[None, :]
    b_base_h = b_base.reshape(1, H)

    shared = dict(wu_t=wu_t, wg_t=wg_t, wx_T=wx_T, wdt_t=wdt_t, wout_T=wout_T,
                  wbase_T=wbase_T, wrouter_T=wrouter_T, acomb_T=acomb_T,
                  bcomb_T=bcomb_T, b_base=b_base_h, conv_w=conv_w_h,
                  conv_b=conv_b_h, dt_bias=dt_bias_h, d_par=d_par_h,
                  a_par=a_par_h, dt_ln_w=dt_ln_h, b_ln_w=b_ln_h, c_ln_w=c_ln_h,
                  ident_f=ident_f, ident_b=ident_b, ones_f=ones_f, cvals=cvals_h)

    nc = build_nc()
    in_maps = []
    for core in range(8):
        b = core if core < 4 else core - 4
        m = dict(shared)
        m["x"] = np.ascontiguousarray(hs[b])
        in_maps.append(m)

    res = run_bass_kernel_spmd(nc, in_maps, core_ids=list(range(8)))
    out = np.stack([np.asarray(res.results[b]["y_out"], np.float32)
                    for b in range(4)], axis=0)
    return out


# version 04
# speedup vs baseline: 1.1106x; 1.0674x over previous
import numpy as np
import ml_dtypes
from contextlib import ExitStack
import concourse.bass as bass
import concourse.mybir as mybir
from concourse.bass_utils import run_bass_kernel_spmd

F32 = mybir.dt.float32
BF16 = mybir.dt.bfloat16
AF = mybir.ActivationFunctionType
ALU = mybir.AluOpType
AX = mybir.AxisListType

H = 4096
D = 8192
NS = 16
R = 256
T = 2048
LN_EPS = 1e-5
RMS_EPS = 1e-6

CONST_VALS = [LN_EPS, RMS_EPS, 1.0, -0.5, -1.0, 1.0 / 4096, 1.0 / 256, 1.0 / 16]

NKT = H // 128
NOT = D // 128
NTT = T // 128
NTB = T // 512


def build_nc():
    nc = bass.Bass("TRN2", target_bir_lowering=False)

    x = nc.dram_tensor("x", [T, H], F32, kind="ExternalInput").ap()
    wu_t = nc.dram_tensor("wu_t", [NOT, NKT, 128, 128], F32, kind="ExternalInput").ap()
    wg_t = nc.dram_tensor("wg_t", [NOT, NKT, 128, 128], F32, kind="ExternalInput").ap()
    wx_T = nc.dram_tensor("wx_T", [D, 288], F32, kind="ExternalInput").ap()
    wdt_t = nc.dram_tensor("wdt_t", [2, NOT, 128, 128], F32, kind="ExternalInput").ap()
    wout_T = nc.dram_tensor("wout_T", [D, H], F32, kind="ExternalInput").ap()
    wbase_T = nc.dram_tensor("wbase_T", [H, H], F32, kind="ExternalInput").ap()
    wrouter_T = nc.dram_tensor("wrouter_T", [H, 8], F32, kind="ExternalInput").ap()
    acomb_T = nc.dram_tensor("acomb_T", [H, 128], F32, kind="ExternalInput").ap()
    bcomb_T = nc.dram_tensor("bcomb_T", [128, H], F32, kind="ExternalInput").ap()
    b_base = nc.dram_tensor("b_base", [1, H], F32, kind="ExternalInput").ap()
    conv_w = nc.dram_tensor("conv_w", [128, NOT * 4], F32, kind="ExternalInput").ap()
    conv_b = nc.dram_tensor("conv_b", [128, NOT], F32, kind="ExternalInput").ap()
    dt_bias = nc.dram_tensor("dt_bias", [128, NOT], F32, kind="ExternalInput").ap()
    d_par = nc.dram_tensor("d_par", [128, NOT], F32, kind="ExternalInput").ap()
    a_par = nc.dram_tensor("a_par", [128, NOT * NS], F32, kind="ExternalInput").ap()
    dt_ln_w = nc.dram_tensor("dt_ln_w", [128, 2], F32, kind="ExternalInput").ap()
    b_ln_w = nc.dram_tensor("b_ln_w", [16, 1], F32, kind="ExternalInput").ap()
    c_ln_w = nc.dram_tensor("c_ln_w", [16, 1], F32, kind="ExternalInput").ap()
    ident_f = nc.dram_tensor("ident_f", [128, 128], F32, kind="ExternalInput").ap()
    ident_b = nc.dram_tensor("ident_b", [128, 128], BF16, kind="ExternalInput").ap()
    ones_f = nc.dram_tensor("ones_f", [1, 128], F32, kind="ExternalInput").ap()
    cvals = nc.dram_tensor("cvals", [128, 16], F32, kind="ExternalInput").ap()

    y_out = nc.dram_tensor("y_out", [T, H], F32, kind="ExternalOutput").ap()

    wu_bf = nc.dram_tensor("wu_bf", [NOT, NKT, 128, 128], BF16, kind="Internal").ap()
    wg_bf = nc.dram_tensor("wg_bf", [NOT, NKT, 128, 128], BF16, kind="Internal").ap()
    wx_bf = nc.dram_tensor("wx_bf", [D, 288], BF16, kind="Internal").ap()
    wdt_bf = nc.dram_tensor("wdt_bf", [2, NOT, 128, 128], BF16, kind="Internal").ap()
    wout_bf = nc.dram_tensor("wout_bf", [D, H], BF16, kind="Internal").ap()
    wbase_bf = nc.dram_tensor("wbase_bf", [H, H], BF16, kind="Internal").ap()
    wrouter_bf = nc.dram_tensor("wrouter_bf", [H, 8], BF16, kind="Internal").ap()
    acomb_bf = nc.dram_tensor("acomb_bf", [H, 128], BF16, kind="Internal").ap()
    bcomb_bf = nc.dram_tensor("bcomb_bf", [128, H], BF16, kind="Internal").ap()
    u_f = nc.dram_tensor("u_f", [D, T], F32, kind="Internal").ap()
    ub_t = nc.dram_tensor("ub_t", [NTT, NOT, 128, 128], BF16, kind="Internal").ap()
    sg_f = nc.dram_tensor("sg_f", [D, T], F32, kind="Internal").ap()
    delta_f = nc.dram_tensor("delta_f", [D, T], F32, kind="Internal").ap()
    yg_td = nc.dram_tensor("yg_td", [NTT, NOT, 128, 128], BF16, kind="Internal").ap()
    x2_f = nc.dram_tensor("x2_f", [T, H], F32, kind="Internal").ap()
    bt_d = nc.dram_tensor("bt_d", [16, T], F32, kind="Internal").ap()
    ct_d = nc.dram_tensor("ct_d", [16, T], F32, kind="Internal").ap()

    G = [0]
    LAST = [None]

    with nc.semaphore("gs") as gs, nc.semaphore("sm_t") as sm_t, \
         nc.semaphore("sm_s") as sm_s, nc.semaphore("sm_v") as sm_v, \
         nc.semaphore("sm_d") as sm_d:

        def fin(inst, k=1):
            inst.then_inc(gs, k)
            G[0] += k

        def wt(eng):
            if G[0] > 0:
                eng.wait_ge(gs, G[0])
            LAST[0] = eng

        def dma(out, in_):
            if G[0] > 0:
                nc.sync.wait_ge(gs, G[0])
            LAST[0] = None
            fin(nc.sync.dma_start(out=out, in_=in_), 16)

        def sca(**kw):
            wt(nc.scalar)
            fin(nc.scalar.activation(**kw))

        def scopy(out, in_):
            wt(nc.scalar)
            fin(nc.scalar.copy(out=out, in_=in_))

        def vts(**kw):
            wt(nc.vector)
            fin(nc.vector.tensor_scalar(**kw))

        def vtt(out, in0, in1, op):
            wt(nc.vector)
            fin(nc.vector.tensor_tensor(out=out, in0=in0, in1=in1, op=op))

        def vstt(**kw):
            wt(nc.vector)
            fin(nc.vector.scalar_tensor_tensor(**kw))

        _es1 = ExitStack()
        c_in = _es1.enter_context(nc.sbuf_tensor("c_in", [128, 8192], F32))
        c_out = _es1.enter_context(nc.sbuf_tensor("c_out", [128, 8192], BF16))
        if True:
            def cast_flat(src, dst, nelem, pat):
                CH = 128 * 8192
                nfull = nelem // CH
                sv = src.rearrange(pat)
                dv = dst.rearrange(pat)
                for c in range(nfull):
                    s3 = sv[c * CH:(c + 1) * CH].rearrange("(p f) -> p f", p=128)
                    d3 = dv[c * CH:(c + 1) * CH].rearrange("(p f) -> p f", p=128)
                    dma(c_in[:], s3)
                    scopy(c_out[:], c_in[:])
                    dma(d3, c_out[:])
                rem = nelem - nfull * CH
                if rem:
                    fr = rem // 128
                    s3 = sv[nfull * CH:].rearrange("(p f) -> p f", p=128)
                    d3 = dv[nfull * CH:].rearrange("(p f) -> p f", p=128)
                    dma(c_in[:, :fr], s3)
                    scopy(c_out[:, :fr], c_in[:, :fr])
                    dma(d3, c_out[:, :fr])

            p2 = "a b -> (a b)"
            cast_flat(wx_T, wx_bf, D * 288, p2)
            cast_flat(wbase_T, wbase_bf, H * H, p2)
            cast_flat(wrouter_T, wrouter_bf, H * 8, p2)
            cast_flat(acomb_T, acomb_bf, H * 128, p2)
            cast_flat(bcomb_T, bcomb_bf, 128 * H, p2)

        _es1.close()
        _es2 = ExitStack()
        wd_in = _es2.enter_context(nc.sbuf_tensor("wd_in", [128, 1024], F32))
        wd_out = _es2.enter_context(nc.sbuf_tensor("wd_out", [128, 1024], BF16))
        s_lnw = _es2.enter_context(nc.sbuf_tensor("s_lnw", [128, 2], F32))
        if True:
            dma(s_lnw[:], dt_ln_w)
            for rk in range(2):
                for g in range(NOT // 8):
                    src = wdt_t[rk, g * 8:(g + 1) * 8].rearrange("t p d -> p t d")
                    dst = wdt_bf[rk, g * 8:(g + 1) * 8].rearrange("t p d -> p t d")
                    dma(wd_in[:].rearrange("p (t d) -> p t d", t=8), src)
                    vts(out=wd_out[:], in0=wd_in[:], scalar1=s_lnw[:, rk:rk + 1],
                        scalar2=None, op0=ALU.mult)
                    dma(dst, wd_out[:].rearrange("p (t d) -> p t d", t=8))

        _es2.close()
        _es3 = ExitStack()
        s_idf = _es3.enter_context(nc.sbuf_tensor("s_idf", [128, 128], F32))
        s_idb = _es3.enter_context(nc.sbuf_tensor("s_idb", [128, 128], BF16))
        s_ones = _es3.enter_context(nc.sbuf_tensor("s_ones", [1, 128], F32))
        col1 = _es3.enter_context(nc.sbuf_tensor("col1", [128, 1], F32))
        col2 = _es3.enter_context(nc.sbuf_tensor("col2", [128, 1], F32))
        col3 = _es3.enter_context(nc.sbuf_tensor("col3", [128, 1], F32))
        col4 = _es3.enter_context(nc.sbuf_tensor("col4", [128, 1], F32))
        s_cv = _es3.enter_context(nc.sbuf_tensor("s_cv", [128, 16], F32))
        if True:
            dma(s_idf[:], ident_f)
            dma(s_idb[:], ident_b)
            dma(s_ones[:], ones_f)
            dma(s_cv[:], cvals)
            for ci, cval in enumerate(CONST_VALS):
                nc.const_aps.register(F32, cval, s_cv[:, ci:ci + 1])

            def layernorm_transpose(src_rows, dstT, p1x, p1sq, p1n, ps_bf):
                dstv = dstT.rearrange("p (k t) -> p k t", k=NKT)
                for tt in range(NTT):
                    dma(p1x[:], src_rows[tt * 128:(tt + 1) * 128, :])
                    wt(nc.vector)
                    fin(nc.vector.tensor_reduce(out=col1[:], in_=p1x[:], axis=AX.X, op=ALU.add))
                    wt(nc.scalar)
                    fin(nc.scalar.mul(col1[:], col1[:], 1.0 / H))
                    sca(out=p1sq[:], in_=p1x[:, 0:2048], func=AF.Square, accum_out=col2[:])
                    sca(out=p1sq[:], in_=p1x[:, 2048:4096], func=AF.Square, accum_out=col4[:])
                    vtt(col2[:], col2[:], col4[:], ALU.add)
                    vts(out=col3[:], in0=col1[:], scalar1=col1[:, 0:1], scalar2=None,
                        op0=ALU.mult)
                    vstt(out=col2[:], in0=col2[:], scalar=1.0 / H, in1=col3[:],
                         op0=ALU.mult, op1=ALU.subtract)
                    sca(out=col2[:], in_=col2[:], func=AF.Ln, bias=LN_EPS)
                    sca(out=col2[:], in_=col2[:], func=AF.Exp, scale=-0.5)
                    vstt(out=p1n[:], in0=p1x[:], scalar=col1[:, 0:1],
                         in1=col2[:, 0:1].broadcast_to([128, H]),
                         op0=ALU.subtract, op1=ALU.mult)
                    for hq in range(8):
                        wt(nc.tensor)
                        for j in range(4):
                            hb = hq * 4 + j
                            inst = nc.tensor.transpose(out=ps_bf[:, j * 128:(j + 1) * 128],
                                                       in_=p1n[:, hb * 128:(hb + 1) * 128],
                                                       identity=s_idb[:])
                        fin(inst)
                        scopy(dstv[:, hq * 4:(hq + 1) * 4, tt * 128:(tt + 1) * 128],
                              ps_bf[:].rearrange("p (j q) -> p j q", j=4))

            with nc.sbuf_tensor("lnxT", [128, NKT * T], BF16) as lnxT:
                _es4 = ExitStack()
                p1x = _es4.enter_context(nc.sbuf_tensor("p1x", [128, H], F32))
                p1sq = _es4.enter_context(nc.sbuf_tensor("p1sq", [128, 2048], F32))
                p1n = _es4.enter_context(nc.sbuf_tensor("p1n", [128, H], BF16))
                ps_bf1 = _es4.enter_context(nc.psum_tensor("ps_bf1", [128, 512], BF16))
                if True:
                    layernorm_transpose(x, lnxT, p1x, p1sq, p1n, ps_bf1)

                _es4.close()
                _es5 = ExitStack()
                s_convw = _es5.enter_context(nc.sbuf_tensor("s_convw", [128, NOT * 4], F32))
                s_convb = _es5.enter_context(nc.sbuf_tensor("s_convb", [128, NOT], F32))
                sw0 = _es5.enter_context(nc.sbuf_tensor("sw0", [128, NKT * 128], BF16))
                sw1 = _es5.enter_context(nc.sbuf_tensor("sw1", [128, NKT * 128], BF16))
                sw_f = _es5.enter_context(nc.sbuf_tensor("sw_f", [128, NKT * 128], F32))
                raw = _es5.enter_context(nc.sbuf_tensor("raw", [128, T + 3], F32))
                su = _es5.enter_context(nc.sbuf_tensor("su", [128, T], F32))
                sub = _es5.enter_context(nc.sbuf_tensor("sub", [128, T], BF16))
                ps_u0 = _es5.enter_context(nc.psum_tensor("ps_u0", [128, 512], F32))
                ps_u1 = _es5.enter_context(nc.psum_tensor("ps_u1", [128, 512], F32))
                if True:
                    dma(s_convw[:], conv_w)
                    dma(s_convb[:], conv_b)
                    lv = lnxT.rearrange("p (k t) -> p k t", k=NKT)
                    G0 = G[0]
                    for eng in (nc.sync, nc.tensor, nc.scalar, nc.vector):
                        eng.wait_ge(gs, G0)
                    ct = [0]; cs = [0]; cv = [0]; cd = [0]
                    waited = {}
                    def lw(eng, tag, sem, val):
                        if val <= 0:
                            return
                        key = (tag, id(sem))
                        if waited.get(key, -1) >= val:
                            return
                        waited[key] = val
                        eng.wait_ge(sem, val)
                    SWB = [sw0, sw1]
                    PSU = [ps_u0, ps_u1]
                    cast_cs = {}; mm_ct = {}; conv_cv = {}; sub_cs = {}
                    out_cd = {}; w_cd = {}; evac3_cs = {}
                    ps_user = [0, 0]
                    def w_dma(idx):
                        if idx >= 2 * NOT:
                            return
                        wsrc2 = wu_t if idx < NOT else wg_t
                        lw(nc.sync, "sy", sm_s, cast_cs.get(idx - 1, 0))
                        inst = nc.sync.dma_start(
                            out=sw_f[:].rearrange("p (k o) -> p k o", k=NKT),
                            in_=wsrc2[idx % NOT].rearrange("k p o -> p k o"))
                        inst.then_inc(sm_d, 16); cd[0] += 16
                        w_cd[idx] = cd[0]

                    w_dma(0)
                    for idx in range(2 * NOT):
                        do_conv = idx < NOT
                        ot = idx % NOT
                        par = idx % 2
                        swc = SWB[par]
                        lw(nc.scalar, "sc", sm_d, w_cd[idx])
                        lw(nc.scalar, "sc", sm_t, mm_ct.get((idx - 2, NTB - 1), 0))
                        inst = nc.scalar.copy(out=swc[:], in_=sw_f[:])
                        inst.then_inc(sm_s, 1); cs[0] += 1
                        cast_cs[idx] = cs[0]
                        w_dma(idx + 1)
                        swv = swc.rearrange("p (k o) -> p k o", k=NKT)
                        for tb in range(NTB):
                            q = tb % 2
                            lw(nc.tensor, "tn", sm_s, max(cast_cs[idx], ps_user[q]))
                            for k in range(NKT):
                                inst = nc.tensor.matmul(
                                    out=PSU[q][:], lhsT=swv[:, k, :],
                                    rhs=lv[:, k, tb * 512:(tb + 1) * 512],
                                    start=(k == 0), stop=(k == NKT - 1))
                            inst.then_inc(sm_t, 1); ct[0] += 1
                            mm_ct[(idx, tb)] = ct[0]
                            lw(nc.scalar, "sc", sm_t, mm_ct[(idx, tb)])
                            if do_conv:
                                lw(nc.scalar, "sc", sm_v, conv_cv.get(idx - 1, 0))
                                inst = nc.scalar.copy(
                                    out=raw[:, 3 + tb * 512:3 + (tb + 1) * 512],
                                    in_=PSU[q][:])
                            else:
                                lw(nc.scalar, "sc", sm_d, out_cd.get(idx - 1, 0))
                                inst = nc.scalar.activation(
                                    out=su[:, tb * 512:(tb + 1) * 512],
                                    in_=PSU[q][:], func=AF.Silu)
                            inst.then_inc(sm_s, 1); cs[0] += 1
                            ps_user[q] = cs[0]
                            if tb == NTB - 1:
                                evac3_cs[idx] = cs[0]
                        if do_conv:
                            lw(nc.vector, "ve", sm_s, evac3_cs[idx])
                            lw(nc.vector, "ve", sm_d, out_cd.get(idx - 1, 0))
                            lw(nc.vector, "ve", sm_s, sub_cs.get(idx - 1, 0))
                            inst = nc.vector.memset(raw[:, 0:3], 0.0)
                            inst.then_inc(sm_v, 1); cv[0] += 1
                            lw(nc.vector, "ve", sm_v, cv[0])
                            inst = nc.vector.tensor_scalar(
                                out=su[:], in0=raw[:, 0:T],
                                scalar1=s_convw[:, ot * 4:ot * 4 + 1],
                                scalar2=None, op0=ALU.mult)
                            inst.then_inc(sm_v, 1); cv[0] += 1
                            for kk in range(1, 4):
                                lw(nc.vector, "ve", sm_v, cv[0])
                                inst = nc.vector.scalar_tensor_tensor(
                                    out=su[:], in0=raw[:, kk:T + kk],
                                    scalar=s_convw[:, ot * 4 + kk:ot * 4 + kk + 1],
                                    in1=su[:], op0=ALU.mult, op1=ALU.add)
                                inst.then_inc(sm_v, 1); cv[0] += 1
                            conv_cv[idx] = cv[0]
                            lw(nc.scalar, "sc", sm_v, conv_cv[idx])
                            inst = nc.scalar.activation(out=su[:], in_=su[:], func=AF.Silu,
                                                        bias=s_convb[:, ot:ot + 1])
                            inst.then_inc(sm_s, 1); cs[0] += 1
                            lw(nc.scalar, "sc", sm_s, cs[0])
                            lw(nc.scalar, "sc", sm_d, out_cd.get(idx - 1, 0))
                            inst = nc.scalar.copy(out=sub[:], in_=su[:])
                            inst.then_inc(sm_s, 1); cs[0] += 1
                            sub_cs[idx] = cs[0]
                            lw(nc.sync, "sy", sm_s, sub_cs[idx])
                            inst = nc.sync.dma_start(out=u_f[ot * 128:(ot + 1) * 128, :],
                                                     in_=su[:])
                            inst.then_inc(sm_d, 16); cd[0] += 16
                            inst = nc.sync.dma_start(
                                out=ub_t[:, ot].rearrange("t p q -> p t q"),
                                in_=sub[:].rearrange("p (t q) -> p t q", t=NTT))
                            inst.then_inc(sm_d, 16); cd[0] += 16
                            out_cd[idx] = cd[0]
                        else:
                            lw(nc.sync, "sy", sm_s, evac3_cs[idx])
                            inst = nc.sync.dma_start(out=sg_f[ot * 128:(ot + 1) * 128, :],
                                                     in_=su[:])
                            inst.then_inc(sm_d, 16); cd[0] += 16
                            out_cd[idx] = cd[0]
                    nc.scalar.wait_ge(sm_t, ct[0])
                    nc.scalar.wait_ge(sm_s, cs[0])
                    nc.scalar.wait_ge(sm_v, cv[0])
                    nc.scalar.wait_ge(sm_d, cd[0])
                    LAST[0] = None
                    inst = nc.scalar.copy(out=col1[:], in_=col1[:])
                    fin(inst)

                _es5.close()
            _es6 = ExitStack()
            ssm_sb = _es6.enter_context(nc.sbuf_tensor("ssm_sb", [128, NTT * 288], F32))
            dtnT_sb = _es6.enter_context(nc.sbuf_tensor("dtnT_sb", [128, 2 * T], BF16))
            BT_sb = _es6.enter_context(nc.sbuf_tensor("BT_sb", [16, T], F32))
            CT_sb = _es6.enter_context(nc.sbuf_tensor("CT_sb", [16, T], F32))
            if True:
                _es7 = ExitStack()
                syg = _es7.enter_context(nc.sbuf_tensor("syg", [128, NOT * 128], BF16))
                swxa = _es7.enter_context(nc.sbuf_tensor("swxa", [128, NOT * 288], BF16))
                ps_s = _es7.enter_context(nc.psum_tensor("ps_s", [128, 512], F32))
                if True:
                    dma(swxa[:].rearrange("p (a n) -> p a n", a=NOT),
                        wx_bf.rearrange("(a p) n -> p a n", p=128))
                    swxv = swxa.rearrange("p (a n) -> p a n", a=NOT)
                    for tt in range(NTT):
                        dma(syg[:].rearrange("p (a q) -> p a q", a=NOT),
                            ub_t[tt].rearrange("a p q -> p a q"))
                        sygv = syg.rearrange("p (a q) -> p a q", a=NOT)
                        wt(nc.tensor)
                        for dk in range(NOT):
                            inst = nc.tensor.matmul(out=ps_s[:, 0:288], lhsT=sygv[:, dk, :],
                                                    rhs=swxv[:, dk, :],
                                                    start=(dk == 0), stop=(dk == NOT - 1))
                        fin(inst)
                        scopy(ssm_sb[:, tt * 288:(tt + 1) * 288], ps_s[:, 0:288])

                _es7.close()
                _es8 = ExitStack()
                s_blnw = _es8.enter_context(nc.sbuf_tensor("s_blnw", [16, 1], F32))
                s_clnw = _es8.enter_context(nc.sbuf_tensor("s_clnw", [16, 1], F32))
                p4s = _es8.enter_context(nc.sbuf_tensor("p4s", [128, 256], F32))
                p4b = _es8.enter_context(nc.sbuf_tensor("p4b", [128, 256], BF16))
                p4f = _es8.enter_context(nc.sbuf_tensor("p4f", [128, 16], F32))
                ps_bf2 = _es8.enter_context(nc.psum_tensor("ps_bf2", [128, 128], BF16))
                ps_t32 = _es8.enter_context(nc.psum_tensor("ps_t32", [128, 128], F32))
                if True:
                    dma(s_blnw[:], b_ln_w)
                    dma(s_clnw[:], c_ln_w)
                    for tt in range(NTT):
                        base = ssm_sb[:, tt * 288:(tt + 1) * 288]
                        sca(out=p4s[:], in_=base[:, 0:256], func=AF.Square, accum_out=col1[:])
                        sca(out=col1[:], in_=col1[:], func=AF.Ln, scale=1.0 / R, bias=RMS_EPS)
                        sca(out=col1[:], in_=col1[:], func=AF.Exp, scale=-0.5)
                        vts(out=p4b[:], in0=base[:, 0:256], scalar1=col1[:, 0:1],
                            scalar2=None, op0=ALU.mult)
                        for rk in range(2):
                            wt(nc.tensor)
                            fin(nc.tensor.transpose(out=ps_bf2[:],
                                                    in_=p4b[:, rk * 128:(rk + 1) * 128],
                                                    identity=s_idb[:]))
                            scopy(dtnT_sb[:, rk * T + tt * 128:rk * T + (tt + 1) * 128],
                                  ps_bf2[:])
                        for off, dstT in ((256, BT_sb), (272, CT_sb)):
                            sca(out=p4f[:], in_=base[:, off:off + 16], func=AF.Square,
                                accum_out=col1[:])
                            sca(out=col1[:], in_=col1[:], func=AF.Ln, scale=1.0 / NS,
                                bias=RMS_EPS)
                            sca(out=col1[:], in_=col1[:], func=AF.Exp, scale=-0.5)
                            vts(out=p4f[:], in0=base[:, off:off + 16], scalar1=col1[:, 0:1],
                                scalar2=None, op0=ALU.mult)
                            wt(nc.tensor)
                            fin(nc.tensor.transpose(out=ps_t32[0:16, :], in_=p4f[:],
                                                    identity=s_idf[:]))
                            scopy(dstT[:, tt * 128:(tt + 1) * 128], ps_t32[0:16, :])
                    vts(out=BT_sb[:], in0=BT_sb[:], scalar1=s_blnw[:, 0:1], scalar2=None,
                        op0=ALU.mult)
                    vts(out=CT_sb[:], in0=CT_sb[:], scalar1=s_clnw[:, 0:1], scalar2=None,
                        op0=ALU.mult)
                    dma(bt_d, BT_sb[:])
                    dma(ct_d, CT_sb[:])

                _es8.close()
                _es9 = ExitStack()
                s_dtb = _es9.enter_context(nc.sbuf_tensor("s_dtb", [128, NOT], F32))
                swdt = _es9.enter_context(nc.sbuf_tensor("swdt", [128, 2 * NOT * 128], BF16))
                d1 = _es9.enter_context(nc.sbuf_tensor("d1", [128, 512], F32))
                d2 = _es9.enter_context(nc.sbuf_tensor("d2", [128, 512], F32))
                d3 = _es9.enter_context(nc.sbuf_tensor("d3", [128, 512], F32))
                ps_d = _es9.enter_context(nc.psum_tensor("ps_d", [128, 512], F32))
                if True:
                    dma(s_dtb[:], dt_bias)
                    dma(swdt[:].rearrange("p (r a d) -> p r a d", r=2, a=NOT),
                        wdt_bf.rearrange("r a p d -> p r a d"))
                    swdtv = swdt.rearrange("p (r a d) -> p r a d", r=2, a=NOT)
                    dtv = dtnT_sb.rearrange("p (r t) -> p r t", r=2)
                    for dc in range(NOT):
                        for tb in range(NTB):
                            wt(nc.tensor)
                            for rk in range(2):
                                inst = nc.tensor.matmul(out=ps_d[:], lhsT=swdtv[:, rk, dc, :],
                                                        rhs=dtv[:, rk, tb * 512:(tb + 1) * 512],
                                                        start=(rk == 0), stop=(rk == 1))
                            fin(inst)
                            bias = s_dtb[:, dc:dc + 1]
                            sca(out=d1[:], in_=ps_d[:], func=AF.Abs, bias=bias)
                            sca(out=d1[:], in_=d1[:], func=AF.Exp, scale=-1.0)
                            sca(out=d1[:], in_=d1[:], func=AF.Ln, bias=1.0)
                            sca(out=d2[:], in_=ps_d[:], func=AF.Relu, bias=bias)
                            vtt(d3[:], d1[:], d2[:], ALU.add)
                            dma(delta_f[dc * 128:(dc + 1) * 128, tb * 512:(tb + 1) * 512],
                                d3[:])

                _es9.close()
            _es6.close()
            _es10 = ExitStack()
            s_apar = _es10.enter_context(nc.sbuf_tensor("s_apar", [128, NOT * NS], F32))
            s_dpar = _es10.enter_context(nc.sbuf_tensor("s_dpar", [128, NOT], F32))
            btrow = _es10.enter_context(nc.sbuf_tensor("btrow", [1, T], F32))
            ctrow = _es10.enter_context(nc.sbuf_tensor("ctrow", [1, T], F32))
            sdel = _es10.enter_context(nc.sbuf_tensor("sdel", [128, 4 * T], F32))
            sy = _es10.enter_context(nc.sbuf_tensor("sy", [128, 4 * T], F32))
            sdu = _es10.enter_context(nc.sbuf_tensor("sdu", [128, 4 * T], F32))
            sda = _es10.enter_context(nc.sbuf_tensor("sda", [128, T], F32))
            sdb = _es10.enter_context(nc.sbuf_tensor("sdb", [128, T], F32))
            sS = _es10.enter_context(nc.sbuf_tensor("sS", [128, T], F32))
            sgt = _es10.enter_context(nc.sbuf_tensor("sgt", [128, T], F32))
            sygb = _es10.enter_context(nc.sbuf_tensor("sygb", [128, T], BF16))
            ps_r1 = _es10.enter_context(nc.psum_tensor("ps_r1", [128, T], F32))
            ps_r2 = _es10.enter_context(nc.psum_tensor("ps_r2", [128, T], F32))
            if True:
                dma(s_apar[:], a_par)
                dma(s_dpar[:], d_par)
                for blk in range(NOT // 4):
                    for j in range(4):
                        dt_i = blk * 4 + j
                        dma(sdel[:, j * T:(j + 1) * T],
                            delta_f[dt_i * 128:(dt_i + 1) * 128, :])
                        dma(sy[:, j * T:(j + 1) * T],
                            u_f[dt_i * 128:(dt_i + 1) * 128, :])
                    for j in range(4):
                        dt_i = blk * 4 + j
                        vtt(sdu[:, j * T:(j + 1) * T], sdel[:, j * T:(j + 1) * T],
                            sy[:, j * T:(j + 1) * T], ALU.mult)
                        vts(out=sy[:, j * T:(j + 1) * T], in0=sy[:, j * T:(j + 1) * T],
                            scalar1=s_dpar[:, dt_i:dt_i + 1], scalar2=None, op0=ALU.mult)
                    for n in range(NS):
                        dma(btrow[:], bt_d[n:n + 1, :])
                        dma(ctrow[:], ct_d[n:n + 1, :])
                        wt(nc.tensor)
                        for s in range(NTB):
                            nc.tensor.matmul(out=ps_r1[:, s * 512:(s + 1) * 512],
                                             lhsT=s_ones[:],
                                             rhs=btrow[0:1, s * 512:(s + 1) * 512],
                                             start=True, stop=True)
                            inst = nc.tensor.matmul(out=ps_r2[:, s * 512:(s + 1) * 512],
                                                    lhsT=s_ones[:],
                                                    rhs=ctrow[0:1, s * 512:(s + 1) * 512],
                                                    start=True, stop=True)
                        fin(inst)
                        for j in range(4):
                            dt_i = blk * 4 + j
                            sca(out=sda[:], in_=sdel[:, j * T:(j + 1) * T], func=AF.Exp,
                                scale=s_apar[:, dt_i * NS + n:dt_i * NS + n + 1])
                            vtt(sdb[:], sdu[:, j * T:(j + 1) * T], ps_r1[:], ALU.mult)
                            wt(nc.vector)
                            fin(nc.vector.tensor_tensor_scan(out=sS[:], data0=sda[:],
                                                             data1=sdb[:], initial=0.0,
                                                             op0=ALU.mult, op1=ALU.add))
                            vtt(sdb[:], sS[:], ps_r2[:], ALU.mult)
                            vtt(sy[:, j * T:(j + 1) * T], sy[:, j * T:(j + 1) * T],
                                sdb[:], ALU.add)
                    for j in range(4):
                        dt_i = blk * 4 + j
                        dma(sgt[:], sg_f[dt_i * 128:(dt_i + 1) * 128, :])
                        vtt(sygb[:], sy[:, j * T:(j + 1) * T], sgt[:], ALU.mult)
                        dma(yg_td[:, dt_i].rearrange("t p q -> p t q"),
                            sygb[:].rearrange("p (t q) -> p t q", t=NTT))

            _es10.close()
            _es11 = ExitStack()
            swo = _es11.enter_context(nc.sbuf_tensor("swo", [128, NOT * 512], BF16))
            swo_f = _es11.enter_context(nc.sbuf_tensor("swo_f", [128, NOT * 256], F32))
            syg2 = _es11.enter_context(nc.sbuf_tensor("syg2", [128, NOT * 128], BF16))
            sxr = _es11.enter_context(nc.sbuf_tensor("sxr", [128, 512], F32))
            sx2 = _es11.enter_context(nc.sbuf_tensor("sx2", [128, 512], F32))
            ps_o = _es11.enter_context(nc.psum_tensor("ps_o", [128, 512], F32))
            if True:
                for hg in range(8):
                    for half in range(2):
                        dma(swo_f[:].rearrange("p (a h) -> p a h", a=NOT // 2),
                            wout_T[half * (D // 2):(half + 1) * (D // 2),
                                   hg * 512:(hg + 1) * 512]
                            .rearrange("(a p) h -> p a h", p=128))
                        scopy(swo[:, half * (NOT // 2) * 512:(half + 1) * (NOT // 2) * 512],
                              swo_f[:])
                    swov = swo.rearrange("p (a h) -> p a h", a=NOT)
                    for tt in range(NTT):
                        dma(syg2[:].rearrange("p (a q) -> p a q", a=NOT),
                            yg_td[tt].rearrange("a p q -> p a q"))
                        sygv2 = syg2.rearrange("p (a q) -> p a q", a=NOT)
                        wt(nc.tensor)
                        for dk in range(NOT):
                            inst = nc.tensor.matmul(out=ps_o[:], lhsT=sygv2[:, dk, :],
                                                    rhs=swov[:, dk, :],
                                                    start=(dk == 0), stop=(dk == NOT - 1))
                        fin(inst)
                        dma(sxr[:], x[tt * 128:(tt + 1) * 128, hg * 512:(hg + 1) * 512])
                        vtt(sx2[:], ps_o[:], sxr[:], ALU.add)
                        dma(x2_f[tt * 128:(tt + 1) * 128, hg * 512:(hg + 1) * 512], sx2[:])

            _es11.close()
            with nc.sbuf_tensor("ln2T", [128, NKT * T], BF16) as ln2T:
                _es12 = ExitStack()
                p1x2 = _es12.enter_context(nc.sbuf_tensor("p1x2", [128, H], F32))
                p1sq2 = _es12.enter_context(nc.sbuf_tensor("p1sq2", [128, 2048], F32))
                p1n2 = _es12.enter_context(nc.sbuf_tensor("p1n2", [128, H], BF16))
                ps_bf3 = _es12.enter_context(nc.psum_tensor("ps_bf3", [128, 512], BF16))
                if True:
                    layernorm_transpose(x2_f, ln2T, p1x2, p1sq2, p1n2, ps_bf3)

                _es12.close()
                _es13 = ExitStack()
                shwT = _es13.enter_context(nc.sbuf_tensor("shwT", [128, NTT * 128], BF16))
                sbc = _es13.enter_context(nc.sbuf_tensor("sbc", [128, H], BF16))
                sbb = _es13.enter_context(nc.sbuf_tensor("sbb", [1, H], F32))
                if True:
                    dma(sbc[:], bcomb_bf)
                    dma(sbb[:], b_base)
                    lv2 = ln2T.rearrange("p (k t) -> p k t", k=NKT)

                    _es14 = ExitStack()
                    swr = _es14.enter_context(nc.sbuf_tensor("swr", [128, NKT * 8], BF16))
                    sac = _es14.enter_context(nc.sbuf_tensor("sac", [128, NKT * 128], BF16))
                    sl = _es14.enter_context(nc.sbuf_tensor("sl", [128, 8], F32))
                    sl2 = _es14.enter_context(nc.sbuf_tensor("sl2", [128, 8], F32))
                    seq1 = _es14.enter_context(nc.sbuf_tensor("seq1", [128, 8], F32))
                    seq2 = _es14.enter_context(nc.sbuf_tensor("seq2", [128, 8], F32))
                    swm = _es14.enter_context(nc.sbuf_tensor("swm", [128, 8], F32))
                    shw = _es14.enter_context(nc.sbuf_tensor("shw", [128, 128], BF16))
                    ps_l = _es14.enter_context(nc.psum_tensor("ps_l", [128, 512], F32))
                    ps_h = _es14.enter_context(nc.psum_tensor("ps_h", [128, 128], F32))
                    ps_bf4 = _es14.enter_context(nc.psum_tensor("ps_bf4", [128, 128], BF16))
                    if True:
                        dma(swr[:].rearrange("p (k e) -> p k e", k=NKT),
                            wrouter_bf.rearrange("(k p) e -> p k e", p=128))
                        dma(sac[:].rearrange("p (k e) -> p k e", k=NKT),
                            acomb_bf.rearrange("(k p) e -> p k e", p=128))
                        swrv = swr.rearrange("p (k e) -> p k e", k=NKT)
                        sacv = sac.rearrange("p (k e) -> p k e", k=NKT)
                        for tt in range(NTT):
                            wt(nc.tensor)
                            for k in range(NKT):
                                inst = nc.tensor.matmul(
                                    out=ps_l[:, 0:8],
                                    lhsT=lv2[:, k, tt * 128:(tt + 1) * 128],
                                    rhs=swrv[:, k, :],
                                    start=(k == 0), stop=(k == NKT - 1))
                            fin(inst)
                            scopy(sl[:], ps_l[:, 0:8])
                            wt(nc.vector)
                            fin(nc.vector.tensor_reduce(out=col1[:], in_=sl[:], axis=AX.X,
                                                        op=ALU.max))
                            vts(out=seq1[:], in0=sl[:], scalar1=col1[:, 0:1], scalar2=None,
                                op0=ALU.is_ge)
                            vstt(out=sl2[:], in0=seq1[:], scalar=-1e30, in1=sl[:],
                                 op0=ALU.mult, op1=ALU.add)
                            wt(nc.vector)
                            fin(nc.vector.tensor_reduce(out=col2[:], in_=sl2[:], axis=AX.X,
                                                        op=ALU.max))
                            vts(out=seq2[:], in0=sl2[:], scalar1=col2[:, 0:1], scalar2=None,
                                op0=ALU.is_ge)
                            vtt(col3[:], col2[:], col1[:], ALU.subtract)
                            sca(out=col3[:], in_=col3[:], func=AF.Exp)
                            vts(out=col3[:], in0=col3[:], scalar1=1.0, scalar2=None,
                                op0=ALU.add)
                            wt(nc.vector)
                            fin(nc.vector.reciprocal(out=col3[:], in_=col3[:]))
                            vts(out=col4[:], in0=col3[:], scalar1=-1.0, scalar2=1.0,
                                op0=ALU.mult, op1=ALU.add)
                            vts(out=swm[:], in0=seq1[:], scalar1=col3[:, 0:1], scalar2=None,
                                op0=ALU.mult)
                            vstt(out=swm[:], in0=seq2[:], scalar=col4[:, 0:1], in1=swm[:],
                                 op0=ALU.mult, op1=ALU.add)
                            vts(out=swm[:], in0=swm[:], scalar1=2.0, scalar2=None,
                                op0=ALU.mult)
                            wt(nc.tensor)
                            for k in range(NKT):
                                inst = nc.tensor.matmul(
                                    out=ps_h[:],
                                    lhsT=lv2[:, k, tt * 128:(tt + 1) * 128],
                                    rhs=sacv[:, k, :],
                                    start=(k == 0), stop=(k == NKT - 1))
                            fin(inst)
                            for e in range(8):
                                vts(out=shw[:, e * 16:(e + 1) * 16],
                                    in0=ps_h[:, e * 16:(e + 1) * 16],
                                    scalar1=swm[:, e:e + 1], scalar2=None, op0=ALU.mult)
                            wt(nc.tensor)
                            fin(nc.tensor.transpose(out=ps_bf4[:], in_=shw[:],
                                                    identity=s_idb[:]))
                            scopy(shwT[:, tt * 128:(tt + 1) * 128], ps_bf4[:])

                    _es14.close()
                    _es15 = ExitStack()
                    swb = _es15.enter_context(nc.sbuf_tensor("swb", [128, NKT * 512], BF16))
                    sbrep = _es15.enter_context(nc.sbuf_tensor("sbrep", [128, 512], F32))
                    sff1 = _es15.enter_context(nc.sbuf_tensor("sff1", [128, 512], F32))
                    sxf = _es15.enter_context(nc.sbuf_tensor("sxf", [128, 512], F32))
                    ps_f = _es15.enter_context(nc.psum_tensor("ps_f", [128, 512], F32))
                    ps_q = _es15.enter_context(nc.psum_tensor("ps_q", [128, 512], F32))
                    if True:
                        for hg in range(8):
                            dma(swb[:].rearrange("p (k h) -> p k h", k=NKT),
                                wbase_bf[:, hg * 512:(hg + 1) * 512]
                                .rearrange("(k p) h -> p k h", p=128))
                            swbv = swb.rearrange("p (k h) -> p k h", k=NKT)
                            wt(nc.tensor)
                            fin(nc.tensor.matmul(out=ps_q[:], lhsT=s_ones[:],
                                                 rhs=sbb[0:1, hg * 512:(hg + 1) * 512],
                                                 start=True, stop=True))
                            scopy(sbrep[:], ps_q[:])
                            for tt in range(NTT):
                                wt(nc.tensor)
                                for k in range(NKT):
                                    nc.tensor.matmul(
                                        out=ps_f[:],
                                        lhsT=lv2[:, k, tt * 128:(tt + 1) * 128],
                                        rhs=swbv[:, k, :],
                                        start=(k == 0), stop=False)
                                inst = nc.tensor.matmul(
                                    out=ps_f[:],
                                    lhsT=shwT[:, tt * 128:(tt + 1) * 128],
                                    rhs=sbc[:, hg * 512:(hg + 1) * 512],
                                    start=False, stop=True)
                                fin(inst)
                                vtt(sff1[:], ps_f[:], sbrep[:], ALU.add)
                                dma(sxf[:], x2_f[tt * 128:(tt + 1) * 128,
                                                 hg * 512:(hg + 1) * 512])
                                vtt(sff1[:], sff1[:], sxf[:], ALU.add)
                                dma(y_out[tt * 128:(tt + 1) * 128,
                                          hg * 512:(hg + 1) * 512], sff1[:])
                    _es15.close()
                _es13.close()
            wt(nc.sync)
        _es3.close()
    return nc


def kernel(**inputs):
    hs = np.asarray(inputs["L0_attn_hidden_states"], np.float32)
    W_in = np.asarray(inputs["L0_attn_W_in"], np.float32)
    conv_w = np.asarray(inputs["L0_attn_conv_w"], np.float32)
    conv_b = np.asarray(inputs["L0_attn_conv_b"], np.float32)
    Wx = np.asarray(inputs["L0_attn_Wx"], np.float32)
    dt_ln_w = np.asarray(inputs["L0_attn_dt_ln_w"], np.float32)
    b_ln_w = np.asarray(inputs["L0_attn_b_ln_w"], np.float32)
    c_ln_w = np.asarray(inputs["L0_attn_c_ln_w"], np.float32)
    Wdt = np.asarray(inputs["L0_attn_Wdt"], np.float32)
    dt_bias = np.asarray(inputs["L0_attn_dt_bias"], np.float32)
    A = np.asarray(inputs["L0_attn_A"], np.float32)
    D_param = np.asarray(inputs["L0_attn_D_param"], np.float32)
    W_out = np.asarray(inputs["L0_attn_W_out"], np.float32)
    W_base = np.asarray(inputs["L0_ffn_W_base"], np.float32)
    b_base = np.asarray(inputs["L0_ffn_b_base"], np.float32)
    W_router = np.asarray(inputs["L0_ffn_W_router"], np.float32)
    A_ffn = np.asarray(inputs["L0_ffn_A"], np.float32)
    B_ffn = np.asarray(inputs["L0_ffn_B"], np.float32)

    wu_t = np.ascontiguousarray(
        W_in[:D].reshape(NOT, 128, NKT, 128).transpose(0, 2, 3, 1))
    wg_t = np.ascontiguousarray(
        W_in[D:].reshape(NOT, 128, NKT, 128).transpose(0, 2, 3, 1))
    wx_T = np.ascontiguousarray(Wx.T)
    wdt_t = np.ascontiguousarray(
        Wdt.T.reshape(2, 128, NOT, 128).transpose(0, 2, 1, 3))
    wout_T = np.ascontiguousarray(W_out.T)
    wbase_T = np.ascontiguousarray(W_base.T)
    wrouter_T = np.ascontiguousarray(W_router.T)
    acomb_T = np.ascontiguousarray(A_ffn.transpose(2, 0, 1).reshape(H, 128))
    bcomb_T = np.ascontiguousarray(B_ffn.transpose(0, 2, 1).reshape(128, H))
    conv_w_h = np.ascontiguousarray(conv_w.reshape(NOT, 128, 4).transpose(1, 0, 2)
                                    .reshape(128, NOT * 4))
    conv_b_h = np.ascontiguousarray(conv_b.reshape(NOT, 128).T)
    dt_bias_h = np.ascontiguousarray(dt_bias.reshape(NOT, 128).T)
    d_par_h = np.ascontiguousarray(D_param.reshape(NOT, 128).T)
    a_par_h = np.ascontiguousarray(A.reshape(NOT, 128, NS).transpose(1, 0, 2)
                                   .reshape(128, NOT * NS))
    dt_ln_h = np.ascontiguousarray(dt_ln_w.reshape(2, 128).T)
    b_ln_h = np.ascontiguousarray(b_ln_w.reshape(16, 1))
    c_ln_h = np.ascontiguousarray(c_ln_w.reshape(16, 1))
    ident_f = np.eye(128, dtype=np.float32)
    ident_b = np.eye(128).astype(ml_dtypes.bfloat16)
    ones_f = np.ones((1, 128), np.float32)
    cvals_h = np.zeros((128, 16), np.float32)
    cvals_h[:, :len(CONST_VALS)] = np.asarray(CONST_VALS, np.float32)[None, :]
    b_base_h = b_base.reshape(1, H)

    shared = dict(wu_t=wu_t, wg_t=wg_t, wx_T=wx_T, wdt_t=wdt_t, wout_T=wout_T,
                  wbase_T=wbase_T, wrouter_T=wrouter_T, acomb_T=acomb_T,
                  bcomb_T=bcomb_T, b_base=b_base_h, conv_w=conv_w_h,
                  conv_b=conv_b_h, dt_bias=dt_bias_h, d_par=d_par_h,
                  a_par=a_par_h, dt_ln_w=dt_ln_h, b_ln_w=b_ln_h, c_ln_w=c_ln_h,
                  ident_f=ident_f, ident_b=ident_b, ones_f=ones_f, cvals=cvals_h)

    nc = build_nc()
    in_maps = []
    for core in range(8):
        b = core if core < 4 else core - 4
        m = dict(shared)
        m["x"] = np.ascontiguousarray(hs[b])
        in_maps.append(m)

    res = run_bass_kernel_spmd(nc, in_maps, core_ids=list(range(8)))
    out = np.stack([np.asarray(res.results[b]["y_out"], np.float32)
                    for b in range(4)], axis=0)
    return out


# version 05 (final)
# speedup vs baseline: 1.2408x; 1.1172x over previous
import numpy as np
import ml_dtypes
from contextlib import ExitStack
import concourse.bass as bass
import concourse.mybir as mybir
from concourse.bass_utils import run_bass_kernel_spmd

F32 = mybir.dt.float32
BF16 = mybir.dt.bfloat16
AF = mybir.ActivationFunctionType
ALU = mybir.AluOpType
AX = mybir.AxisListType

H = 4096
D = 8192
NS = 16
R = 256
T = 2048
LN_EPS = 1e-5
RMS_EPS = 1e-6

CONST_VALS = [LN_EPS, RMS_EPS, 1.0, -0.5, -1.0, 1.0 / 4096, 1.0 / 256, 1.0 / 16]

NKT = H // 128
NOT = D // 128
NTT = T // 128
NTB = T // 512


def build_nc():
    nc = bass.Bass("TRN2", target_bir_lowering=False)

    x = nc.dram_tensor("x", [T, H], F32, kind="ExternalInput").ap()
    wu_t = nc.dram_tensor("wu_t", [NOT, NKT, 128, 128], F32, kind="ExternalInput").ap()
    wg_t = nc.dram_tensor("wg_t", [NOT, NKT, 128, 128], F32, kind="ExternalInput").ap()
    wx_T = nc.dram_tensor("wx_T", [D, 288], F32, kind="ExternalInput").ap()
    wdt_t = nc.dram_tensor("wdt_t", [2, NOT, 128, 128], F32, kind="ExternalInput").ap()
    wout_T = nc.dram_tensor("wout_T", [D, H], F32, kind="ExternalInput").ap()
    wbase_T = nc.dram_tensor("wbase_T", [H, H], F32, kind="ExternalInput").ap()
    wrouter_T = nc.dram_tensor("wrouter_T", [H, 8], F32, kind="ExternalInput").ap()
    acomb_T = nc.dram_tensor("acomb_T", [H, 128], F32, kind="ExternalInput").ap()
    bcomb_T = nc.dram_tensor("bcomb_T", [128, H], F32, kind="ExternalInput").ap()
    b_base = nc.dram_tensor("b_base", [1, H], F32, kind="ExternalInput").ap()
    conv_w = nc.dram_tensor("conv_w", [128, NOT * 4], F32, kind="ExternalInput").ap()
    conv_b = nc.dram_tensor("conv_b", [128, NOT], F32, kind="ExternalInput").ap()
    dt_bias = nc.dram_tensor("dt_bias", [128, NOT], F32, kind="ExternalInput").ap()
    d_par = nc.dram_tensor("d_par", [128, NOT], F32, kind="ExternalInput").ap()
    a_par = nc.dram_tensor("a_par", [128, NOT * NS], F32, kind="ExternalInput").ap()
    dt_ln_w = nc.dram_tensor("dt_ln_w", [128, 2], F32, kind="ExternalInput").ap()
    b_ln_w = nc.dram_tensor("b_ln_w", [16, 1], F32, kind="ExternalInput").ap()
    c_ln_w = nc.dram_tensor("c_ln_w", [16, 1], F32, kind="ExternalInput").ap()
    ident_f = nc.dram_tensor("ident_f", [128, 128], F32, kind="ExternalInput").ap()
    ident_b = nc.dram_tensor("ident_b", [128, 128], BF16, kind="ExternalInput").ap()
    ones_f = nc.dram_tensor("ones_f", [1, 128], F32, kind="ExternalInput").ap()
    cvals = nc.dram_tensor("cvals", [128, 16], F32, kind="ExternalInput").ap()

    y_out = nc.dram_tensor("y_out", [T, H], F32, kind="ExternalOutput").ap()

    wu_bf = nc.dram_tensor("wu_bf", [NOT, NKT, 128, 128], BF16, kind="Internal").ap()
    wg_bf = nc.dram_tensor("wg_bf", [NOT, NKT, 128, 128], BF16, kind="Internal").ap()
    wx_bf = nc.dram_tensor("wx_bf", [D, 288], BF16, kind="Internal").ap()
    wdt_bf = nc.dram_tensor("wdt_bf", [2, NOT, 128, 128], BF16, kind="Internal").ap()
    wout_bf = nc.dram_tensor("wout_bf", [D, H], BF16, kind="Internal").ap()
    wbase_bf = nc.dram_tensor("wbase_bf", [H, H], BF16, kind="Internal").ap()
    wrouter_bf = nc.dram_tensor("wrouter_bf", [H, 8], BF16, kind="Internal").ap()
    acomb_bf = nc.dram_tensor("acomb_bf", [H, 128], BF16, kind="Internal").ap()
    bcomb_bf = nc.dram_tensor("bcomb_bf", [128, H], BF16, kind="Internal").ap()
    u_f = nc.dram_tensor("u_f", [D, T], F32, kind="Internal").ap()
    ub_t = nc.dram_tensor("ub_t", [NTT, NOT, 128, 128], BF16, kind="Internal").ap()
    sg_f = nc.dram_tensor("sg_f", [D, T], F32, kind="Internal").ap()
    delta_f = nc.dram_tensor("delta_f", [D, T], F32, kind="Internal").ap()
    yg_td = nc.dram_tensor("yg_td", [NTT, NOT, 128, 128], BF16, kind="Internal").ap()
    x2_f = nc.dram_tensor("x2_f", [T, H], F32, kind="Internal").ap()
    bt_d = nc.dram_tensor("bt_d", [16, T], F32, kind="Internal").ap()
    ct_d = nc.dram_tensor("ct_d", [16, T], F32, kind="Internal").ap()

    G = [0]
    LAST = [None]

    with nc.semaphore("gs") as gs, nc.semaphore("sm_t") as sm_t, \
         nc.semaphore("sm_s") as sm_s, nc.semaphore("sm_v") as sm_v, \
         nc.semaphore("sm_d") as sm_d:

        def fin(inst, k=1):
            inst.then_inc(gs, k)
            G[0] += k

        def wt(eng):
            if G[0] > 0:
                eng.wait_ge(gs, G[0])
            LAST[0] = eng

        def dma(out, in_):
            if G[0] > 0:
                nc.sync.wait_ge(gs, G[0])
            LAST[0] = None
            fin(nc.sync.dma_start(out=out, in_=in_), 16)

        def sca(**kw):
            wt(nc.scalar)
            fin(nc.scalar.activation(**kw))

        def scopy(out, in_):
            wt(nc.scalar)
            fin(nc.scalar.copy(out=out, in_=in_))

        def vts(**kw):
            wt(nc.vector)
            fin(nc.vector.tensor_scalar(**kw))

        def vtt(out, in0, in1, op):
            wt(nc.vector)
            fin(nc.vector.tensor_tensor(out=out, in0=in0, in1=in1, op=op))

        def vstt(**kw):
            wt(nc.vector)
            fin(nc.vector.scalar_tensor_tensor(**kw))

        _es1 = ExitStack()
        c_in = _es1.enter_context(nc.sbuf_tensor("c_in", [128, 8192], F32))
        c_out = _es1.enter_context(nc.sbuf_tensor("c_out", [128, 8192], BF16))
        if True:
            def cast_flat(src, dst, nelem, pat):
                CH = 128 * 8192
                nfull = nelem // CH
                sv = src.rearrange(pat)
                dv = dst.rearrange(pat)
                for c in range(nfull):
                    s3 = sv[c * CH:(c + 1) * CH].rearrange("(p f) -> p f", p=128)
                    d3 = dv[c * CH:(c + 1) * CH].rearrange("(p f) -> p f", p=128)
                    dma(c_in[:], s3)
                    scopy(c_out[:], c_in[:])
                    dma(d3, c_out[:])
                rem = nelem - nfull * CH
                if rem:
                    fr = rem // 128
                    s3 = sv[nfull * CH:].rearrange("(p f) -> p f", p=128)
                    d3 = dv[nfull * CH:].rearrange("(p f) -> p f", p=128)
                    dma(c_in[:, :fr], s3)
                    scopy(c_out[:, :fr], c_in[:, :fr])
                    dma(d3, c_out[:, :fr])

            p2 = "a b -> (a b)"
            cast_flat(wx_T, wx_bf, D * 288, p2)
            cast_flat(wbase_T, wbase_bf, H * H, p2)
            cast_flat(wrouter_T, wrouter_bf, H * 8, p2)
            cast_flat(acomb_T, acomb_bf, H * 128, p2)
            cast_flat(bcomb_T, bcomb_bf, 128 * H, p2)

        _es1.close()
        _es2 = ExitStack()
        wd_in = _es2.enter_context(nc.sbuf_tensor("wd_in", [128, 1024], F32))
        wd_out = _es2.enter_context(nc.sbuf_tensor("wd_out", [128, 1024], BF16))
        s_lnw = _es2.enter_context(nc.sbuf_tensor("s_lnw", [128, 2], F32))
        if True:
            dma(s_lnw[:], dt_ln_w)
            for rk in range(2):
                for g in range(NOT // 8):
                    src = wdt_t[rk, g * 8:(g + 1) * 8].rearrange("t p d -> p t d")
                    dst = wdt_bf[rk, g * 8:(g + 1) * 8].rearrange("t p d -> p t d")
                    dma(wd_in[:].rearrange("p (t d) -> p t d", t=8), src)
                    vts(out=wd_out[:], in0=wd_in[:], scalar1=s_lnw[:, rk:rk + 1],
                        scalar2=None, op0=ALU.mult)
                    dma(dst, wd_out[:].rearrange("p (t d) -> p t d", t=8))

        _es2.close()
        _es3 = ExitStack()
        s_idf = _es3.enter_context(nc.sbuf_tensor("s_idf", [128, 128], F32))
        s_idb = _es3.enter_context(nc.sbuf_tensor("s_idb", [128, 128], BF16))
        s_ones = _es3.enter_context(nc.sbuf_tensor("s_ones", [1, 128], F32))
        col1 = _es3.enter_context(nc.sbuf_tensor("col1", [128, 1], F32))
        col2 = _es3.enter_context(nc.sbuf_tensor("col2", [128, 1], F32))
        col3 = _es3.enter_context(nc.sbuf_tensor("col3", [128, 1], F32))
        col4 = _es3.enter_context(nc.sbuf_tensor("col4", [128, 1], F32))
        s_cv = _es3.enter_context(nc.sbuf_tensor("s_cv", [128, 16], F32))
        if True:
            dma(s_idf[:], ident_f)
            dma(s_idb[:], ident_b)
            dma(s_ones[:], ones_f)
            dma(s_cv[:], cvals)
            for ci, cval in enumerate(CONST_VALS):
                nc.const_aps.register(F32, cval, s_cv[:, ci:ci + 1])

            def layernorm_transpose(src_rows, dstT, p1x, p1sq, p1n, ps_bf):
                dstv = dstT.rearrange("p (k t) -> p k t", k=NKT)
                for tt in range(NTT):
                    dma(p1x[:], src_rows[tt * 128:(tt + 1) * 128, :])
                    wt(nc.vector)
                    fin(nc.vector.tensor_reduce(out=col1[:], in_=p1x[:], axis=AX.X, op=ALU.add))
                    wt(nc.scalar)
                    fin(nc.scalar.mul(col1[:], col1[:], 1.0 / H))
                    sca(out=p1sq[:], in_=p1x[:, 0:2048], func=AF.Square, accum_out=col2[:])
                    sca(out=p1sq[:], in_=p1x[:, 2048:4096], func=AF.Square, accum_out=col4[:])
                    vtt(col2[:], col2[:], col4[:], ALU.add)
                    vts(out=col3[:], in0=col1[:], scalar1=col1[:, 0:1], scalar2=None,
                        op0=ALU.mult)
                    vstt(out=col2[:], in0=col2[:], scalar=1.0 / H, in1=col3[:],
                         op0=ALU.mult, op1=ALU.subtract)
                    sca(out=col2[:], in_=col2[:], func=AF.Ln, bias=LN_EPS)
                    sca(out=col2[:], in_=col2[:], func=AF.Exp, scale=-0.5)
                    vstt(out=p1n[:], in0=p1x[:], scalar=col1[:, 0:1],
                         in1=col2[:, 0:1].broadcast_to([128, H]),
                         op0=ALU.subtract, op1=ALU.mult)
                    for hq in range(8):
                        wt(nc.tensor)
                        for j in range(4):
                            hb = hq * 4 + j
                            inst = nc.tensor.transpose(out=ps_bf[:, j * 128:(j + 1) * 128],
                                                       in_=p1n[:, hb * 128:(hb + 1) * 128],
                                                       identity=s_idb[:])
                        fin(inst)
                        scopy(dstv[:, hq * 4:(hq + 1) * 4, tt * 128:(tt + 1) * 128],
                              ps_bf[:].rearrange("p (j q) -> p j q", j=4))

            with nc.sbuf_tensor("lnxT", [128, NKT * T], BF16) as lnxT:
                _es4 = ExitStack()
                p1x = _es4.enter_context(nc.sbuf_tensor("p1x", [128, H], F32))
                p1sq = _es4.enter_context(nc.sbuf_tensor("p1sq", [128, 2048], F32))
                p1n = _es4.enter_context(nc.sbuf_tensor("p1n", [128, H], BF16))
                ps_bf1 = _es4.enter_context(nc.psum_tensor("ps_bf1", [128, 512], BF16))
                if True:
                    layernorm_transpose(x, lnxT, p1x, p1sq, p1n, ps_bf1)

                _es4.close()
                _es5 = ExitStack()
                s_convw = _es5.enter_context(nc.sbuf_tensor("s_convw", [128, NOT * 4], F32))
                s_convb = _es5.enter_context(nc.sbuf_tensor("s_convb", [128, NOT], F32))
                sw0 = _es5.enter_context(nc.sbuf_tensor("sw0", [128, NKT * 128], BF16))
                sw1 = _es5.enter_context(nc.sbuf_tensor("sw1", [128, NKT * 128], BF16))
                sw_f = _es5.enter_context(nc.sbuf_tensor("sw_f", [128, NKT * 128], F32))
                raw = _es5.enter_context(nc.sbuf_tensor("raw", [128, T + 3], F32))
                su = _es5.enter_context(nc.sbuf_tensor("su", [128, T], F32))
                sub = _es5.enter_context(nc.sbuf_tensor("sub", [128, T], BF16))
                ps_u0 = _es5.enter_context(nc.psum_tensor("ps_u0", [128, 512], F32))
                ps_u1 = _es5.enter_context(nc.psum_tensor("ps_u1", [128, 512], F32))
                if True:
                    dma(s_convw[:], conv_w)
                    dma(s_convb[:], conv_b)
                    lv = lnxT.rearrange("p (k t) -> p k t", k=NKT)
                    G0 = G[0]
                    for eng in (nc.sync, nc.tensor, nc.scalar, nc.vector):
                        eng.wait_ge(gs, G0)
                    ct = [0]; cs = [0]; cv = [0]; cd = [0]
                    waited = {}
                    def lw(eng, tag, sem, val):
                        if val <= 0:
                            return
                        key = (tag, id(sem))
                        if waited.get(key, -1) >= val:
                            return
                        waited[key] = val
                        eng.wait_ge(sem, val)
                    SWB = [sw0, sw1]
                    PSU = [ps_u0, ps_u1]
                    cast_cs = {}; mm_ct = {}; conv_cv = {}; sub_cs = {}
                    out_cd = {}; w_cd = {}; evac3_cs = {}
                    ps_user = [0, 0]
                    def w_dma(idx):
                        if idx >= 2 * NOT:
                            return
                        wsrc2 = wu_t if idx < NOT else wg_t
                        lw(nc.sync, "sy", sm_s, cast_cs.get(idx - 1, 0))
                        inst = nc.sync.dma_start(
                            out=sw_f[:].rearrange("p (k o) -> p k o", k=NKT),
                            in_=wsrc2[idx % NOT].rearrange("k p o -> p k o"))
                        inst.then_inc(sm_d, 16); cd[0] += 16
                        w_cd[idx] = cd[0]

                    w_dma(0)
                    for idx in range(2 * NOT):
                        do_conv = idx < NOT
                        ot = idx % NOT
                        par = idx % 2
                        swc = SWB[par]
                        lw(nc.scalar, "sc", sm_d, w_cd[idx])
                        lw(nc.scalar, "sc", sm_t, mm_ct.get((idx - 2, NTB - 1), 0))
                        inst = nc.scalar.copy(out=swc[:], in_=sw_f[:])
                        inst.then_inc(sm_s, 1); cs[0] += 1
                        cast_cs[idx] = cs[0]
                        w_dma(idx + 1)
                        swv = swc.rearrange("p (k o) -> p k o", k=NKT)
                        for tb in range(NTB):
                            q = tb % 2
                            lw(nc.tensor, "tn", sm_s, max(cast_cs[idx], ps_user[q]))
                            for k in range(NKT):
                                inst = nc.tensor.matmul(
                                    out=PSU[q][:], lhsT=swv[:, k, :],
                                    rhs=lv[:, k, tb * 512:(tb + 1) * 512],
                                    start=(k == 0), stop=(k == NKT - 1))
                            inst.then_inc(sm_t, 1); ct[0] += 1
                            mm_ct[(idx, tb)] = ct[0]
                            lw(nc.scalar, "sc", sm_t, mm_ct[(idx, tb)])
                            if do_conv:
                                lw(nc.scalar, "sc", sm_v, conv_cv.get(idx - 1, 0))
                                inst = nc.scalar.copy(
                                    out=raw[:, 3 + tb * 512:3 + (tb + 1) * 512],
                                    in_=PSU[q][:])
                            else:
                                lw(nc.scalar, "sc", sm_d, out_cd.get(idx - 1, 0))
                                inst = nc.scalar.activation(
                                    out=su[:, tb * 512:(tb + 1) * 512],
                                    in_=PSU[q][:], func=AF.Silu)
                            inst.then_inc(sm_s, 1); cs[0] += 1
                            ps_user[q] = cs[0]
                            if tb == NTB - 1:
                                evac3_cs[idx] = cs[0]
                        if do_conv:
                            lw(nc.vector, "ve", sm_s, evac3_cs[idx])
                            lw(nc.vector, "ve", sm_d, out_cd.get(idx - 1, 0))
                            lw(nc.vector, "ve", sm_s, sub_cs.get(idx - 1, 0))
                            inst = nc.vector.memset(raw[:, 0:3], 0.0)
                            inst.then_inc(sm_v, 1); cv[0] += 1
                            lw(nc.vector, "ve", sm_v, cv[0])
                            inst = nc.vector.tensor_scalar(
                                out=su[:], in0=raw[:, 0:T],
                                scalar1=s_convw[:, ot * 4:ot * 4 + 1],
                                scalar2=None, op0=ALU.mult)
                            inst.then_inc(sm_v, 1); cv[0] += 1
                            for kk in range(1, 4):
                                lw(nc.vector, "ve", sm_v, cv[0])
                                inst = nc.vector.scalar_tensor_tensor(
                                    out=su[:], in0=raw[:, kk:T + kk],
                                    scalar=s_convw[:, ot * 4 + kk:ot * 4 + kk + 1],
                                    in1=su[:], op0=ALU.mult, op1=ALU.add)
                                inst.then_inc(sm_v, 1); cv[0] += 1
                            conv_cv[idx] = cv[0]
                            lw(nc.scalar, "sc", sm_v, conv_cv[idx])
                            inst = nc.scalar.activation(out=su[:], in_=su[:], func=AF.Silu,
                                                        bias=s_convb[:, ot:ot + 1])
                            inst.then_inc(sm_s, 1); cs[0] += 1
                            lw(nc.scalar, "sc", sm_s, cs[0])
                            lw(nc.scalar, "sc", sm_d, out_cd.get(idx - 1, 0))
                            inst = nc.scalar.copy(out=sub[:], in_=su[:])
                            inst.then_inc(sm_s, 1); cs[0] += 1
                            sub_cs[idx] = cs[0]
                            lw(nc.sync, "sy", sm_s, sub_cs[idx])
                            inst = nc.sync.dma_start(out=u_f[ot * 128:(ot + 1) * 128, :],
                                                     in_=su[:])
                            inst.then_inc(sm_d, 16); cd[0] += 16
                            inst = nc.sync.dma_start(
                                out=ub_t[:, ot].rearrange("t p q -> p t q"),
                                in_=sub[:].rearrange("p (t q) -> p t q", t=NTT))
                            inst.then_inc(sm_d, 16); cd[0] += 16
                            out_cd[idx] = cd[0]
                        else:
                            lw(nc.sync, "sy", sm_s, evac3_cs[idx])
                            inst = nc.sync.dma_start(out=sg_f[ot * 128:(ot + 1) * 128, :],
                                                     in_=su[:])
                            inst.then_inc(sm_d, 16); cd[0] += 16
                            out_cd[idx] = cd[0]
                    nc.scalar.wait_ge(sm_t, ct[0])
                    nc.scalar.wait_ge(sm_s, cs[0])
                    nc.scalar.wait_ge(sm_v, cv[0])
                    nc.scalar.wait_ge(sm_d, cd[0])
                    LAST[0] = None
                    inst = nc.scalar.copy(out=col1[:], in_=col1[:])
                    fin(inst)

                _es5.close()
            _es6 = ExitStack()
            ssm_sb = _es6.enter_context(nc.sbuf_tensor("ssm_sb", [128, NTT * 288], F32))
            dtnT_sb = _es6.enter_context(nc.sbuf_tensor("dtnT_sb", [128, 2 * T], BF16))
            BT_sb = _es6.enter_context(nc.sbuf_tensor("BT_sb", [16, T], F32))
            CT_sb = _es6.enter_context(nc.sbuf_tensor("CT_sb", [16, T], F32))
            if True:
                _es7 = ExitStack()
                syg = _es7.enter_context(nc.sbuf_tensor("syg", [128, NOT * 128], BF16))
                swxa = _es7.enter_context(nc.sbuf_tensor("swxa", [128, NOT * 288], BF16))
                ps_s = _es7.enter_context(nc.psum_tensor("ps_s", [128, 512], F32))
                if True:
                    dma(swxa[:].rearrange("p (a n) -> p a n", a=NOT),
                        wx_bf.rearrange("(a p) n -> p a n", p=128))
                    swxv = swxa.rearrange("p (a n) -> p a n", a=NOT)
                    for tt in range(NTT):
                        dma(syg[:].rearrange("p (a q) -> p a q", a=NOT),
                            ub_t[tt].rearrange("a p q -> p a q"))
                        sygv = syg.rearrange("p (a q) -> p a q", a=NOT)
                        wt(nc.tensor)
                        for dk in range(NOT):
                            inst = nc.tensor.matmul(out=ps_s[:, 0:288], lhsT=sygv[:, dk, :],
                                                    rhs=swxv[:, dk, :],
                                                    start=(dk == 0), stop=(dk == NOT - 1))
                        fin(inst)
                        scopy(ssm_sb[:, tt * 288:(tt + 1) * 288], ps_s[:, 0:288])

                _es7.close()
                _es8 = ExitStack()
                s_blnw = _es8.enter_context(nc.sbuf_tensor("s_blnw", [16, 1], F32))
                s_clnw = _es8.enter_context(nc.sbuf_tensor("s_clnw", [16, 1], F32))
                p4s = _es8.enter_context(nc.sbuf_tensor("p4s", [128, 256], F32))
                p4b = _es8.enter_context(nc.sbuf_tensor("p4b", [128, 256], BF16))
                p4f = _es8.enter_context(nc.sbuf_tensor("p4f", [128, 16], F32))
                ps_bf2 = _es8.enter_context(nc.psum_tensor("ps_bf2", [128, 128], BF16))
                ps_t32 = _es8.enter_context(nc.psum_tensor("ps_t32", [128, 128], F32))
                if True:
                    dma(s_blnw[:], b_ln_w)
                    dma(s_clnw[:], c_ln_w)
                    for tt in range(NTT):
                        base = ssm_sb[:, tt * 288:(tt + 1) * 288]
                        sca(out=p4s[:], in_=base[:, 0:256], func=AF.Square, accum_out=col1[:])
                        sca(out=col1[:], in_=col1[:], func=AF.Ln, scale=1.0 / R, bias=RMS_EPS)
                        sca(out=col1[:], in_=col1[:], func=AF.Exp, scale=-0.5)
                        vts(out=p4b[:], in0=base[:, 0:256], scalar1=col1[:, 0:1],
                            scalar2=None, op0=ALU.mult)
                        for rk in range(2):
                            wt(nc.tensor)
                            fin(nc.tensor.transpose(out=ps_bf2[:],
                                                    in_=p4b[:, rk * 128:(rk + 1) * 128],
                                                    identity=s_idb[:]))
                            scopy(dtnT_sb[:, rk * T + tt * 128:rk * T + (tt + 1) * 128],
                                  ps_bf2[:])
                        for off, dstT in ((256, BT_sb), (272, CT_sb)):
                            sca(out=p4f[:], in_=base[:, off:off + 16], func=AF.Square,
                                accum_out=col1[:])
                            sca(out=col1[:], in_=col1[:], func=AF.Ln, scale=1.0 / NS,
                                bias=RMS_EPS)
                            sca(out=col1[:], in_=col1[:], func=AF.Exp, scale=-0.5)
                            vts(out=p4f[:], in0=base[:, off:off + 16], scalar1=col1[:, 0:1],
                                scalar2=None, op0=ALU.mult)
                            wt(nc.tensor)
                            fin(nc.tensor.transpose(out=ps_t32[0:16, :], in_=p4f[:],
                                                    identity=s_idf[:]))
                            scopy(dstT[:, tt * 128:(tt + 1) * 128], ps_t32[0:16, :])
                    vts(out=BT_sb[:], in0=BT_sb[:], scalar1=s_blnw[:, 0:1], scalar2=None,
                        op0=ALU.mult)
                    vts(out=CT_sb[:], in0=CT_sb[:], scalar1=s_clnw[:, 0:1], scalar2=None,
                        op0=ALU.mult)
                    dma(bt_d, BT_sb[:])
                    dma(ct_d, CT_sb[:])

                _es8.close()
                _es9 = ExitStack()
                s_dtb = _es9.enter_context(nc.sbuf_tensor("s_dtb", [128, NOT], F32))
                swdt = _es9.enter_context(nc.sbuf_tensor("swdt", [128, 2 * NOT * 128], BF16))
                d1 = _es9.enter_context(nc.sbuf_tensor("d1", [128, 512], F32))
                d2 = _es9.enter_context(nc.sbuf_tensor("d2", [128, 512], F32))
                d3 = _es9.enter_context(nc.sbuf_tensor("d3", [128, 512], F32))
                ps_d = _es9.enter_context(nc.psum_tensor("ps_d", [128, 512], F32))
                if True:
                    dma(s_dtb[:], dt_bias)
                    dma(swdt[:].rearrange("p (r a d) -> p r a d", r=2, a=NOT),
                        wdt_bf.rearrange("r a p d -> p r a d"))
                    swdtv = swdt.rearrange("p (r a d) -> p r a d", r=2, a=NOT)
                    dtv = dtnT_sb.rearrange("p (r t) -> p r t", r=2)
                    for dc in range(NOT):
                        for tb in range(NTB):
                            wt(nc.tensor)
                            for rk in range(2):
                                inst = nc.tensor.matmul(out=ps_d[:], lhsT=swdtv[:, rk, dc, :],
                                                        rhs=dtv[:, rk, tb * 512:(tb + 1) * 512],
                                                        start=(rk == 0), stop=(rk == 1))
                            fin(inst)
                            bias = s_dtb[:, dc:dc + 1]
                            sca(out=d1[:], in_=ps_d[:], func=AF.Abs, bias=bias)
                            sca(out=d1[:], in_=d1[:], func=AF.Exp, scale=-1.0)
                            sca(out=d1[:], in_=d1[:], func=AF.Ln, bias=1.0)
                            sca(out=d2[:], in_=ps_d[:], func=AF.Relu, bias=bias)
                            vtt(d3[:], d1[:], d2[:], ALU.add)
                            dma(delta_f[dc * 128:(dc + 1) * 128, tb * 512:(tb + 1) * 512],
                                d3[:])

                _es9.close()
            _es6.close()
            _es10 = ExitStack()
            s_apar = _es10.enter_context(nc.sbuf_tensor("s_apar", [128, NOT * NS], F32))
            s_dpar = _es10.enter_context(nc.sbuf_tensor("s_dpar", [128, NOT], F32))
            btrow = _es10.enter_context(nc.sbuf_tensor("btrow", [1, T], F32))
            ctrow = _es10.enter_context(nc.sbuf_tensor("ctrow", [1, T], F32))
            sdel = _es10.enter_context(nc.sbuf_tensor("sdel", [128, 4 * T], F32))
            sy = _es10.enter_context(nc.sbuf_tensor("sy", [128, 4 * T], F32))
            sdu = _es10.enter_context(nc.sbuf_tensor("sdu", [128, 4 * T], F32))
            sda = _es10.enter_context(nc.sbuf_tensor("sda", [128, T], F32))
            sda2 = _es10.enter_context(nc.sbuf_tensor("sda2", [128, T], F32))
            btrow2 = _es10.enter_context(nc.sbuf_tensor("btrow2", [1, T], F32))
            ctrow2 = _es10.enter_context(nc.sbuf_tensor("ctrow2", [1, T], F32))
            sdb = _es10.enter_context(nc.sbuf_tensor("sdb", [128, T], F32))
            sS = _es10.enter_context(nc.sbuf_tensor("sS", [128, T], F32))
            sgt = _es10.enter_context(nc.sbuf_tensor("sgt", [128, T], F32))
            sygb = _es10.enter_context(nc.sbuf_tensor("sygb", [128, T], BF16))
            ps_r1 = _es10.enter_context(nc.psum_tensor("ps_r1", [128, T], F32))
            ps_r2 = _es10.enter_context(nc.psum_tensor("ps_r2", [128, T], F32))
            if True:
                dma(s_apar[:], a_par)
                dma(s_dpar[:], d_par)
                G0b = G[0]
                for eng in (nc.sync, nc.tensor, nc.scalar, nc.vector):
                    eng.wait_ge(gs, G0b)
                SDA = [sda, sda2]
                BTR = [btrow, btrow2]
                CTR = [ctrow, ctrow2]
                row_cd = {}; rep_ct = {}; da_cs = {}; last_rep_ct = [0]
                scan_cv = {}; repread_cv = [0]; du_cv = {}
                blkload_cd = [0]; dA_read_cv = [0, 0]
                yg_cv = {}; sg_cd = {}; ygdma_cd = {}; last_ygdma = [0]
                last_yg_cv = [0]; last_da_cs = [0]
                for blk in range(NOT // 4):
                    lw(nc.sync, "sy", sm_s, last_da_cs[0])
                    lw(nc.sync, "sy", sm_v, last_yg_cv[0])
                    for j in range(4):
                        dt_i = blk * 4 + j
                        inst = nc.sync.dma_start(out=sdel[:, j * T:(j + 1) * T],
                                                 in_=delta_f[dt_i * 128:(dt_i + 1) * 128, :])
                        inst.then_inc(sm_d, 16); cd[0] += 16
                        inst = nc.sync.dma_start(out=sy[:, j * T:(j + 1) * T],
                                                 in_=u_f[dt_i * 128:(dt_i + 1) * 128, :])
                        inst.then_inc(sm_d, 16); cd[0] += 16
                    blkload_cd[0] = cd[0]
                    lw(nc.vector, "ve", sm_d, blkload_cd[0])
                    for j in range(4):
                        dt_i = blk * 4 + j
                        lw(nc.vector, "ve", sm_v, cv[0])
                        inst = nc.vector.tensor_tensor(
                            out=sdu[:, j * T:(j + 1) * T], in0=sdel[:, j * T:(j + 1) * T],
                            in1=sy[:, j * T:(j + 1) * T], op=ALU.mult)
                        inst.then_inc(sm_v, 1); cv[0] += 1
                        lw(nc.vector, "ve", sm_v, cv[0])
                        inst = nc.vector.tensor_scalar(
                            out=sy[:, j * T:(j + 1) * T], in0=sy[:, j * T:(j + 1) * T],
                            scalar1=s_dpar[:, dt_i:dt_i + 1], scalar2=None, op0=ALU.mult)
                        inst.then_inc(sm_v, 1); cv[0] += 1

                    def row_dma(n):
                        if n >= NS:
                            return
                        pr = n % 2
                        lw(nc.sync, "sy", sm_t, last_rep_ct[0])
                        inst = nc.sync.dma_start(out=BTR[pr][:], in_=bt_d[n:n + 1, :])
                        inst.then_inc(sm_d, 16); cd[0] += 16
                        inst = nc.sync.dma_start(out=CTR[pr][:], in_=ct_d[n:n + 1, :])
                        inst.then_inc(sm_d, 16); cd[0] += 16
                        row_cd[n] = cd[0]

                    row_dma(0)
                    for n in range(NS):
                        pr = n % 2
                        lw(nc.tensor, "tn", sm_d, row_cd[n])
                        lw(nc.tensor, "tn", sm_v, repread_cv[0])
                        for s in range(NTB):
                            nc.tensor.matmul(out=ps_r1[:, s * 512:(s + 1) * 512],
                                             lhsT=s_ones[:],
                                             rhs=BTR[pr][0:1, s * 512:(s + 1) * 512],
                                             start=True, stop=True)
                            inst = nc.tensor.matmul(out=ps_r2[:, s * 512:(s + 1) * 512],
                                                    lhsT=s_ones[:],
                                                    rhs=CTR[pr][0:1, s * 512:(s + 1) * 512],
                                                    start=True, stop=True)
                        inst.then_inc(sm_t, 1); ct[0] += 1
                        rep_ct[n] = ct[0]; last_rep_ct[0] = ct[0]
                        row_dma(n + 1)
                        for j in range(4):
                            dt_i = blk * 4 + j
                            jp = j % 2
                            lw(nc.scalar, "sc", sm_d, blkload_cd[0])
                            lw(nc.scalar, "sc", sm_v, dA_read_cv[jp])
                            inst = nc.scalar.activation(
                                out=SDA[jp][:], in_=sdel[:, j * T:(j + 1) * T], func=AF.Exp,
                                scale=s_apar[:, dt_i * NS + n:dt_i * NS + n + 1])
                            inst.then_inc(sm_s, 1); cs[0] += 1
                            da_cs[(n, j)] = cs[0]
                            last_da_cs[0] = cs[0]
                            lw(nc.vector, "ve", sm_t, rep_ct[n])
                            lw(nc.vector, "ve", sm_v, cv[0])
                            inst = nc.vector.tensor_tensor(
                                out=sdb[:], in0=sdu[:, j * T:(j + 1) * T], in1=ps_r1[:],
                                op=ALU.mult)
                            inst.then_inc(sm_v, 1); cv[0] += 1
                            lw(nc.vector, "ve", sm_s, da_cs[(n, j)])
                            lw(nc.vector, "ve", sm_v, cv[0])
                            inst = nc.vector.tensor_tensor_scan(
                                out=sS[:], data0=SDA[jp][:], data1=sdb[:], initial=0.0,
                                op0=ALU.mult, op1=ALU.add)
                            inst.then_inc(sm_v, 1); cv[0] += 1
                            dA_read_cv[jp] = cv[0]
                            lw(nc.vector, "ve", sm_v, cv[0])
                            inst = nc.vector.tensor_tensor(
                                out=sdb[:], in0=sS[:], in1=ps_r2[:], op=ALU.mult)
                            inst.then_inc(sm_v, 1); cv[0] += 1
                            lw(nc.vector, "ve", sm_v, cv[0])
                            inst = nc.vector.tensor_tensor(
                                out=sy[:, j * T:(j + 1) * T], in0=sy[:, j * T:(j + 1) * T],
                                in1=sdb[:], op=ALU.add)
                            inst.then_inc(sm_v, 1); cv[0] += 1
                            if j == 3:
                                repread_cv[0] = cv[0]
                    for j in range(4):
                        dt_i = blk * 4 + j
                        lw(nc.sync, "sy", sm_v, last_yg_cv[0])
                        inst = nc.sync.dma_start(out=sgt[:],
                                                 in_=sg_f[dt_i * 128:(dt_i + 1) * 128, :])
                        inst.then_inc(sm_d, 16); cd[0] += 16
                        sg_cd[j] = cd[0]
                        lw(nc.vector, "ve", sm_d, max(sg_cd[j], last_ygdma[0]))
                        lw(nc.vector, "ve", sm_v, cv[0])
                        inst = nc.vector.tensor_tensor(
                            out=sygb[:], in0=sy[:, j * T:(j + 1) * T], in1=sgt[:],
                            op=ALU.mult)
                        inst.then_inc(sm_v, 1); cv[0] += 1
                        yg_cv[j] = cv[0]
                        last_yg_cv[0] = cv[0]
                        lw(nc.sync, "sy", sm_v, yg_cv[j])
                        inst = nc.sync.dma_start(
                            out=yg_td[:, dt_i].rearrange("t p q -> p t q"),
                            in_=sygb[:].rearrange("p (t q) -> p t q", t=NTT))
                        inst.then_inc(sm_d, 16); cd[0] += 16
                        ygdma_cd[j] = cd[0]; last_ygdma[0] = cd[0]
                nc.scalar.wait_ge(sm_t, ct[0])
                nc.scalar.wait_ge(sm_s, cs[0])
                nc.scalar.wait_ge(sm_v, cv[0])
                nc.scalar.wait_ge(sm_d, cd[0])
                LAST[0] = None
                inst = nc.scalar.copy(out=col1[:], in_=col1[:])
                fin(inst)

            _es10.close()
            _es11 = ExitStack()
            swo = _es11.enter_context(nc.sbuf_tensor("swo", [128, NOT * 512], BF16))
            swo_f = _es11.enter_context(nc.sbuf_tensor("swo_f", [128, NOT * 256], F32))
            syg2 = _es11.enter_context(nc.sbuf_tensor("syg2", [128, NOT * 128], BF16))
            sxr = _es11.enter_context(nc.sbuf_tensor("sxr", [128, 512], F32))
            sx2 = _es11.enter_context(nc.sbuf_tensor("sx2", [128, 512], F32))
            ps_o = _es11.enter_context(nc.psum_tensor("ps_o", [128, 512], F32))
            if True:
                for hg in range(8):
                    for half in range(2):
                        dma(swo_f[:].rearrange("p (a h) -> p a h", a=NOT // 2),
                            wout_T[half * (D // 2):(half + 1) * (D // 2),
                                   hg * 512:(hg + 1) * 512]
                            .rearrange("(a p) h -> p a h", p=128))
                        scopy(swo[:, half * (NOT // 2) * 512:(half + 1) * (NOT // 2) * 512],
                              swo_f[:])
                    swov = swo.rearrange("p (a h) -> p a h", a=NOT)
                    for tt in range(NTT):
                        dma(syg2[:].rearrange("p (a q) -> p a q", a=NOT),
                            yg_td[tt].rearrange("a p q -> p a q"))
                        sygv2 = syg2.rearrange("p (a q) -> p a q", a=NOT)
                        wt(nc.tensor)
                        for dk in range(NOT):
                            inst = nc.tensor.matmul(out=ps_o[:], lhsT=sygv2[:, dk, :],
                                                    rhs=swov[:, dk, :],
                                                    start=(dk == 0), stop=(dk == NOT - 1))
                        fin(inst)
                        dma(sxr[:], x[tt * 128:(tt + 1) * 128, hg * 512:(hg + 1) * 512])
                        vtt(sx2[:], ps_o[:], sxr[:], ALU.add)
                        dma(x2_f[tt * 128:(tt + 1) * 128, hg * 512:(hg + 1) * 512], sx2[:])

            _es11.close()
            with nc.sbuf_tensor("ln2T", [128, NKT * T], BF16) as ln2T:
                _es12 = ExitStack()
                p1x2 = _es12.enter_context(nc.sbuf_tensor("p1x2", [128, H], F32))
                p1sq2 = _es12.enter_context(nc.sbuf_tensor("p1sq2", [128, 2048], F32))
                p1n2 = _es12.enter_context(nc.sbuf_tensor("p1n2", [128, H], BF16))
                ps_bf3 = _es12.enter_context(nc.psum_tensor("ps_bf3", [128, 512], BF16))
                if True:
                    layernorm_transpose(x2_f, ln2T, p1x2, p1sq2, p1n2, ps_bf3)

                _es12.close()
                _es13 = ExitStack()
                shwT = _es13.enter_context(nc.sbuf_tensor("shwT", [128, NTT * 128], BF16))
                sbc = _es13.enter_context(nc.sbuf_tensor("sbc", [128, H], BF16))
                sbb = _es13.enter_context(nc.sbuf_tensor("sbb", [1, H], F32))
                if True:
                    dma(sbc[:], bcomb_bf)
                    dma(sbb[:], b_base)
                    lv2 = ln2T.rearrange("p (k t) -> p k t", k=NKT)

                    _es14 = ExitStack()
                    swr = _es14.enter_context(nc.sbuf_tensor("swr", [128, NKT * 8], BF16))
                    sac = _es14.enter_context(nc.sbuf_tensor("sac", [128, NKT * 128], BF16))
                    sl = _es14.enter_context(nc.sbuf_tensor("sl", [128, 8], F32))
                    sl2 = _es14.enter_context(nc.sbuf_tensor("sl2", [128, 8], F32))
                    seq1 = _es14.enter_context(nc.sbuf_tensor("seq1", [128, 8], F32))
                    seq2 = _es14.enter_context(nc.sbuf_tensor("seq2", [128, 8], F32))
                    swm = _es14.enter_context(nc.sbuf_tensor("swm", [128, 8], F32))
                    shw = _es14.enter_context(nc.sbuf_tensor("shw", [128, 128], BF16))
                    ps_l = _es14.enter_context(nc.psum_tensor("ps_l", [128, 512], F32))
                    ps_h = _es14.enter_context(nc.psum_tensor("ps_h", [128, 128], F32))
                    ps_bf4 = _es14.enter_context(nc.psum_tensor("ps_bf4", [128, 128], BF16))
                    if True:
                        dma(swr[:].rearrange("p (k e) -> p k e", k=NKT),
                            wrouter_bf.rearrange("(k p) e -> p k e", p=128))
                        dma(sac[:].rearrange("p (k e) -> p k e", k=NKT),
                            acomb_bf.rearrange("(k p) e -> p k e", p=128))
                        swrv = swr.rearrange("p (k e) -> p k e", k=NKT)
                        sacv = sac.rearrange("p (k e) -> p k e", k=NKT)
                        for tt in range(NTT):
                            wt(nc.tensor)
                            for k in range(NKT):
                                inst = nc.tensor.matmul(
                                    out=ps_l[:, 0:8],
                                    lhsT=lv2[:, k, tt * 128:(tt + 1) * 128],
                                    rhs=swrv[:, k, :],
                                    start=(k == 0), stop=(k == NKT - 1))
                            fin(inst)
                            scopy(sl[:], ps_l[:, 0:8])
                            wt(nc.vector)
                            fin(nc.vector.tensor_reduce(out=col1[:], in_=sl[:], axis=AX.X,
                                                        op=ALU.max))
                            vts(out=seq1[:], in0=sl[:], scalar1=col1[:, 0:1], scalar2=None,
                                op0=ALU.is_ge)
                            vstt(out=sl2[:], in0=seq1[:], scalar=-1e30, in1=sl[:],
                                 op0=ALU.mult, op1=ALU.add)
                            wt(nc.vector)
                            fin(nc.vector.tensor_reduce(out=col2[:], in_=sl2[:], axis=AX.X,
                                                        op=ALU.max))
                            vts(out=seq2[:], in0=sl2[:], scalar1=col2[:, 0:1], scalar2=None,
                                op0=ALU.is_ge)
                            vtt(col3[:], col2[:], col1[:], ALU.subtract)
                            sca(out=col3[:], in_=col3[:], func=AF.Exp)
                            vts(out=col3[:], in0=col3[:], scalar1=1.0, scalar2=None,
                                op0=ALU.add)
                            wt(nc.vector)
                            fin(nc.vector.reciprocal(out=col3[:], in_=col3[:]))
                            vts(out=col4[:], in0=col3[:], scalar1=-1.0, scalar2=1.0,
                                op0=ALU.mult, op1=ALU.add)
                            vts(out=swm[:], in0=seq1[:], scalar1=col3[:, 0:1], scalar2=None,
                                op0=ALU.mult)
                            vstt(out=swm[:], in0=seq2[:], scalar=col4[:, 0:1], in1=swm[:],
                                 op0=ALU.mult, op1=ALU.add)
                            vts(out=swm[:], in0=swm[:], scalar1=2.0, scalar2=None,
                                op0=ALU.mult)
                            wt(nc.tensor)
                            for k in range(NKT):
                                inst = nc.tensor.matmul(
                                    out=ps_h[:],
                                    lhsT=lv2[:, k, tt * 128:(tt + 1) * 128],
                                    rhs=sacv[:, k, :],
                                    start=(k == 0), stop=(k == NKT - 1))
                            fin(inst)
                            for e in range(8):
                                vts(out=shw[:, e * 16:(e + 1) * 16],
                                    in0=ps_h[:, e * 16:(e + 1) * 16],
                                    scalar1=swm[:, e:e + 1], scalar2=None, op0=ALU.mult)
                            wt(nc.tensor)
                            fin(nc.tensor.transpose(out=ps_bf4[:], in_=shw[:],
                                                    identity=s_idb[:]))
                            scopy(shwT[:, tt * 128:(tt + 1) * 128], ps_bf4[:])

                    _es14.close()
                    _es15 = ExitStack()
                    swb = _es15.enter_context(nc.sbuf_tensor("swb", [128, NKT * 512], BF16))
                    sbrep = _es15.enter_context(nc.sbuf_tensor("sbrep", [128, 512], F32))
                    sff1 = _es15.enter_context(nc.sbuf_tensor("sff1", [128, 512], F32))
                    sxf = _es15.enter_context(nc.sbuf_tensor("sxf", [128, 512], F32))
                    ps_f = _es15.enter_context(nc.psum_tensor("ps_f", [128, 512], F32))
                    ps_q = _es15.enter_context(nc.psum_tensor("ps_q", [128, 512], F32))
                    if True:
                        for hg in range(8):
                            dma(swb[:].rearrange("p (k h) -> p k h", k=NKT),
                                wbase_bf[:, hg * 512:(hg + 1) * 512]
                                .rearrange("(k p) h -> p k h", p=128))
                            swbv = swb.rearrange("p (k h) -> p k h", k=NKT)
                            wt(nc.tensor)
                            fin(nc.tensor.matmul(out=ps_q[:], lhsT=s_ones[:],
                                                 rhs=sbb[0:1, hg * 512:(hg + 1) * 512],
                                                 start=True, stop=True))
                            scopy(sbrep[:], ps_q[:])
                            for tt in range(NTT):
                                wt(nc.tensor)
                                for k in range(NKT):
                                    nc.tensor.matmul(
                                        out=ps_f[:],
                                        lhsT=lv2[:, k, tt * 128:(tt + 1) * 128],
                                        rhs=swbv[:, k, :],
                                        start=(k == 0), stop=False)
                                inst = nc.tensor.matmul(
                                    out=ps_f[:],
                                    lhsT=shwT[:, tt * 128:(tt + 1) * 128],
                                    rhs=sbc[:, hg * 512:(hg + 1) * 512],
                                    start=False, stop=True)
                                fin(inst)
                                vtt(sff1[:], ps_f[:], sbrep[:], ALU.add)
                                dma(sxf[:], x2_f[tt * 128:(tt + 1) * 128,
                                                 hg * 512:(hg + 1) * 512])
                                vtt(sff1[:], sff1[:], sxf[:], ALU.add)
                                dma(y_out[tt * 128:(tt + 1) * 128,
                                          hg * 512:(hg + 1) * 512], sff1[:])
                    _es15.close()
                _es13.close()
            wt(nc.sync)
        _es3.close()
    return nc


def kernel(**inputs):
    hs = np.asarray(inputs["L0_attn_hidden_states"], np.float32)
    W_in = np.asarray(inputs["L0_attn_W_in"], np.float32)
    conv_w = np.asarray(inputs["L0_attn_conv_w"], np.float32)
    conv_b = np.asarray(inputs["L0_attn_conv_b"], np.float32)
    Wx = np.asarray(inputs["L0_attn_Wx"], np.float32)
    dt_ln_w = np.asarray(inputs["L0_attn_dt_ln_w"], np.float32)
    b_ln_w = np.asarray(inputs["L0_attn_b_ln_w"], np.float32)
    c_ln_w = np.asarray(inputs["L0_attn_c_ln_w"], np.float32)
    Wdt = np.asarray(inputs["L0_attn_Wdt"], np.float32)
    dt_bias = np.asarray(inputs["L0_attn_dt_bias"], np.float32)
    A = np.asarray(inputs["L0_attn_A"], np.float32)
    D_param = np.asarray(inputs["L0_attn_D_param"], np.float32)
    W_out = np.asarray(inputs["L0_attn_W_out"], np.float32)
    W_base = np.asarray(inputs["L0_ffn_W_base"], np.float32)
    b_base = np.asarray(inputs["L0_ffn_b_base"], np.float32)
    W_router = np.asarray(inputs["L0_ffn_W_router"], np.float32)
    A_ffn = np.asarray(inputs["L0_ffn_A"], np.float32)
    B_ffn = np.asarray(inputs["L0_ffn_B"], np.float32)

    wu_t = np.ascontiguousarray(
        W_in[:D].reshape(NOT, 128, NKT, 128).transpose(0, 2, 3, 1))
    wg_t = np.ascontiguousarray(
        W_in[D:].reshape(NOT, 128, NKT, 128).transpose(0, 2, 3, 1))
    wx_T = np.ascontiguousarray(Wx.T)
    wdt_t = np.ascontiguousarray(
        Wdt.T.reshape(2, 128, NOT, 128).transpose(0, 2, 1, 3))
    wout_T = np.ascontiguousarray(W_out.T)
    wbase_T = np.ascontiguousarray(W_base.T)
    wrouter_T = np.ascontiguousarray(W_router.T)
    acomb_T = np.ascontiguousarray(A_ffn.transpose(2, 0, 1).reshape(H, 128))
    bcomb_T = np.ascontiguousarray(B_ffn.transpose(0, 2, 1).reshape(128, H))
    conv_w_h = np.ascontiguousarray(conv_w.reshape(NOT, 128, 4).transpose(1, 0, 2)
                                    .reshape(128, NOT * 4))
    conv_b_h = np.ascontiguousarray(conv_b.reshape(NOT, 128).T)
    dt_bias_h = np.ascontiguousarray(dt_bias.reshape(NOT, 128).T)
    d_par_h = np.ascontiguousarray(D_param.reshape(NOT, 128).T)
    a_par_h = np.ascontiguousarray(A.reshape(NOT, 128, NS).transpose(1, 0, 2)
                                   .reshape(128, NOT * NS))
    dt_ln_h = np.ascontiguousarray(dt_ln_w.reshape(2, 128).T)
    b_ln_h = np.ascontiguousarray(b_ln_w.reshape(16, 1))
    c_ln_h = np.ascontiguousarray(c_ln_w.reshape(16, 1))
    ident_f = np.eye(128, dtype=np.float32)
    ident_b = np.eye(128).astype(ml_dtypes.bfloat16)
    ones_f = np.ones((1, 128), np.float32)
    cvals_h = np.zeros((128, 16), np.float32)
    cvals_h[:, :len(CONST_VALS)] = np.asarray(CONST_VALS, np.float32)[None, :]
    b_base_h = b_base.reshape(1, H)

    shared = dict(wu_t=wu_t, wg_t=wg_t, wx_T=wx_T, wdt_t=wdt_t, wout_T=wout_T,
                  wbase_T=wbase_T, wrouter_T=wrouter_T, acomb_T=acomb_T,
                  bcomb_T=bcomb_T, b_base=b_base_h, conv_w=conv_w_h,
                  conv_b=conv_b_h, dt_bias=dt_bias_h, d_par=d_par_h,
                  a_par=a_par_h, dt_ln_w=dt_ln_h, b_ln_w=b_ln_h, c_ln_w=c_ln_h,
                  ident_f=ident_f, ident_b=ident_b, ones_f=ones_f, cvals=cvals_h)

    nc = build_nc()
    in_maps = []
    for core in range(8):
        b = core if core < 4 else core - 4
        m = dict(shared)
        m["x"] = np.ascontiguousarray(hs[b])
        in_maps.append(m)

    res = run_bass_kernel_spmd(nc, in_maps, core_ids=list(range(8)))
    out = np.stack([np.asarray(res.results[b]["y_out"], np.float32)
                    for b in range(4)], axis=0)
    return out
